# version 03 (final)
# baseline (speedup 1.0000x reference)
import contextlib

import numpy as np
import ml_dtypes

import concourse.bass as bass
import concourse.mybir as mybir
from concourse.bass_utils import run_bass_kernel_spmd

F32 = mybir.dt.float32
BF16 = mybir.dt.bfloat16
AF = mybir.ActivationFunctionType
ALU = mybir.AluOpType
AX = mybir.AxisListType
BF = ml_dtypes.bfloat16

B, T, D = 8, 4096, 2048
H, DKH, DVH, MH = 16, 128, 128, 64
NE, DI, DSH = 16, 1024, 2048 * 2
NT = T // 128
NQ = 4
TQ = T // NQ
NTQ = TQ // 128
KC = D // 128
KI = DI // 128
EPS = 1e-5
CLAMP = 32.0
ROPE_BASE = 10000.0

ENGS = ("sync", "scalar", "vector", "tensor", "gpsimd")


class Tok:
    __slots__ = ("key", "val")

    def __init__(self, key, val):
        self.key = key
        self.val = val


class Prog:

    def __init__(self):
        self.ops = {e: [] for e in ENGS}
        self.cnt = {e: 0 for e in ENGS}
        self.dcnt = {e: 0 for e in ENGS}

    def op(self, eng, method, *args, waits=(), inc=True, **kw):
        tok = None
        if inc:
            self.cnt[eng] += 1
            tok = Tok(("e", eng), self.cnt[eng])
        self.ops[eng].append(("op", method, args, kw, [w for w in waits if w], tok))
        return tok

    def dma(self, eng, out, in_, waits=(), **kw):
        self.dcnt[eng] += 16
        tok = Tok(("d", eng), self.dcnt[eng])
        self.ops[eng].append(("dma", None, (out, in_), kw, [w for w in waits if w], tok))
        return tok

    def bump(self, tok):
        return Tok(tok.key, tok.val + 16)

    def wait(self, eng, waits):
        self.ops[eng].append(("wait", None, None, None, [w for w in waits if w], None))

    def barrier(self):
        toks = [Tok(("e", e), self.cnt[e]) for e in ENGS if self.cnt[e]] + \
               [Tok(("d", e), self.dcnt[e]) for e in ENGS if self.dcnt[e]]
        for e in ENGS:
            self.wait(e, toks)

    def emit(self, block, sems, dsems):
        for eng in ENGS:
            ops = self.ops[eng]
            if not ops:
                continue

            def body(h, ops=ops, eng=eng):
                waited = {}
                for kind, method, args, kw, waits, tok in ops:
                    for w in waits:
                        if waited.get(w.key, 0) >= w.val:
                            continue
                        sem = sems[w.key[1]] if w.key[0] == "e" else dsems[w.key[1]]
                        h.wait_ge(sem, w.val)
                        waited[w.key] = w.val
                    if kind == "wait":
                        continue
                    if kind == "dma":
                        h.dma_start(out=args[0], in_=args[1], **kw).then_inc(dsems[eng], 16)
                    else:
                        inst = getattr(h, method)(*args, **kw)
                        if tok is not None:
                            inst.then_inc(sems[eng], 1)

            getattr(block, eng)(body)


def host_constants():
    pos = np.arange(T, dtype=np.float64)
    inv_freq = 1.0 / (ROPE_BASE ** (np.arange(0, DKH, 2, dtype=np.float64) / DKH))
    ang = pos[:, None] * inv_freq[None, :]
    cos = np.cos(ang).astype(np.float32).T
    sin = np.sin(ang).astype(np.float32).T
    cos128 = np.concatenate([cos, cos], axis=0)
    sin128 = np.concatenate([sin, sin], axis=0)
    qs = float(DKH) ** -0.5
    sin_pm = np.concatenate([sin, -sin], axis=0)
    swap = np.zeros((128, 128), np.float32)
    swap[np.arange(128), np.arange(128) ^ 64] = 1.0
    return {
        "c_cos": cos128.astype(BF),
        "c_sin": sin_pm.astype(BF),
        "c_cosq": (cos128 * qs).astype(BF),
        "c_sinq": (sin_pm * qs).astype(BF),
        "c_swap": swap.astype(BF),
        "c_tri": np.triu(np.ones((128, 128), np.float32)).astype(BF),
        "c_trif": np.triu(np.ones((128, 128), np.float32)),
        "c_iota16": np.tile(np.arange(16, dtype=np.float32)[None, :] * 1e-7, (128, 1)),
        "c_iota4": np.tile(np.arange(4, dtype=np.float32)[None, :] * 1e-7, (128, 1)),
        "c_ones_row": np.ones((1, 128), np.float32).astype(BF),
        "c_ones_rowf": np.ones((1, 128), np.float32),
        "c_ones_col": np.ones((128, 1), np.float32).astype(BF),
        "c_ident": np.eye(128, dtype=np.float32).astype(BF),
        "c_identf": np.eye(128, dtype=np.float32),
    }


CONSTS = host_constants()


class Banks:

    def __init__(self, arena):
        self.a = arena
        self.free = {b: [] for b in range(8)}

    def waits(self, b0, nb):
        w = []
        for b in range(b0, b0 + nb):
            w += self.free[b]
        return w

    def set_free(self, b0, nb, toks):
        for b in range(b0, b0 + nb):
            self.free[b] = list(toks)

    def f32(self, b0, nb):
        v = self.a[:, b0:b0 + nb, :]
        return v.rearrange("p a b -> p (a b)")

    def bf16(self, b0, nb):
        return self.f32(b0, nb).bitcast(BF16)


class KB:
    def __init__(self, stages=("ffn1", "attn", "ffn2"), taps=()):
        self.stages = stages
        self.taps = set(taps)
        nc = bass.Bass("TRN2", target_bir_lowering=False)
        self.nc = nc
        self.p = Prog()
        self.es = contextlib.ExitStack()
        d = {}
        d["x"] = nc.dram_tensor("x", [T, D], F32, kind="ExternalInput").ap()
        for nm, shp in [
            ("wq", [D, D]), ("wk", [D, D]), ("wv", [D, D]), ("wg", [D, D]),
            ("wo", [D, D]), ("ws1", [16, D]), ("ws2", [H * MH, 16]),
            ("bs2", [H * MH]), ("gnw", [DVH]),
            ("rw", [NE, D]), ("ebias", [NE]),
            ("wup", [NE, D, DI]), ("wdown", [NE, DI, D]),
            ("shup", [D, DSH]), ("shdown", [DSH, D]),
        ]:
            d[nm] = nc.dram_tensor(nm, shp, F32, kind="ExternalInput").ap()
        for nm, arr in CONSTS.items():
            dt = BF16 if arr.dtype == BF else F32
            d[nm] = nc.dram_tensor(nm, list(arr.shape), dt, kind="ExternalInput").ap()
        d["x1"] = nc.dram_tensor("x1", [T, D], F32, kind="Internal").ap()
        d["x2"] = nc.dram_tensor("x2", [T, D], F32, kind="Internal").ap()
        d["ln2T"] = nc.dram_tensor("ln2T", [KC, 128, T], BF16, kind="Internal").ap()
        d["ogT"] = nc.dram_tensor("ogT", [H, 128, T], BF16, kind="Internal").ap()
        d["wupb"] = nc.dram_tensor("wupb", [NE, D, DI], BF16, kind="Internal").ap()
        d["wdownb"] = nc.dram_tensor("wdownb", [NE, DI, D], BF16, kind="Internal").ap()
        d["shupb"] = nc.dram_tensor("shupb", [D, DSH], BF16, kind="Internal").ap()
        d["shdownb"] = nc.dram_tensor("shdownb", [DSH, D], BF16, kind="Internal").ap()
        for nm in ("wqT", "wkT", "wvT", "wgT", "woT"):
            d[nm] = nc.dram_tensor(nm, [D, D], BF16, kind="Internal").ap()
        d["out"] = nc.dram_tensor("out", [T, D], F32, kind="ExternalOutput").ap()
        self.d = d
        self.dbg = {}
        self.wb = {}

    def mktap(self, name, shape, dtype=F32):
        if name in self.taps and name not in self.dbg:
            self.dbg[name] = self.nc.dram_tensor(
                "dbg_" + name, list(shape), dtype, kind="ExternalOutput").ap()
        return self.dbg.get(name)

    def build(self):
        nc, p, d = self.nc, self.p, self.d
        with self.es:
            A = lambda *a, **k: self.es.enter_context(nc.sbuf_tensor(*a, **k))
            cs = {}
            ctoks = {}
            for nm in ("c_tri", "c_trif", "c_iota16", "c_iota4", "c_ones_row",
                       "c_ones_rowf", "c_ones_col", "c_ident", "c_identf",
                       "c_swap"):
                arr = CONSTS[nm]
                dt = BF16 if arr.dtype == BF else F32
                cs[nm] = A("s" + nm, list(arr.shape), dt)
                ctoks[nm] = p.dma("sync", cs[nm][:], d[nm])
            self.cs = cs
            self.ctoks = ctoks
            s_eps = A("s_eps", [128, 1], F32)
            self.t_eps = p.op("vector", "memset", s_eps[:], float(EPS))
            self.s_eps = s_eps

            with nc.psum_tensor("pp_init", [128, 8, 512], F32) as pa:
                bk = Banks(pa)
                s_rwraw = A("s_rwraw", [16, D], BF16)
                t0 = p.dma("gpsimd", s_rwraw[:], d["rw"])
                s_rwT = A("s_rwT", [128, KC, 16], BF16)
                prw = bk.bf16(0, 1).rearrange("p (a b) -> p a b", a=8)
                t_prev = None
                for k in range(KC):
                    tt = p.op("tensor", "transpose", prw[:, k % 8, 0:16],
                              s_rwraw[:, k * 128:(k + 1) * 128],
                              cs["c_ident"][0:16, 0:16],
                              waits=[t0, ctoks["c_ident"]] +
                                    ([t_prev] if k >= 8 else []))
                    t_prev = p.op("vector", "tensor_copy", s_rwT[:, k, :],
                                  prw[:, k % 8, 0:16], waits=[tt])
                self.s_rwT = s_rwT
                self.t_rwT = t_prev

                s_eb = A("s_eb", [1, 16], BF16)
                t_eb = p.dma("gpsimd", s_eb[:], d["ebias"].unsqueeze(0))
                s_bmi = A("s_bmi", [128, 16], F32)
                peb = bk.f32(1, 1)
                tm = p.op("tensor", "matmul", peb[:, 0:16], cs["c_ones_row"][:],
                          s_eb[:], start=True, stop=True,
                          waits=[t_eb, ctoks["c_ones_row"], t_prev])
                self.t_bmi = p.op("vector", "tensor_tensor", s_bmi[:],
                                  peb[:, 0:16], cs["c_iota16"][:],
                                  op=ALU.subtract, waits=[tm, ctoks["c_iota16"]])
                self.s_bmi = s_bmi
            p.barrier()

            if "ffn1" in self.stages:
                dst = d["x1"] if ("attn" in self.stages or "ffn2" in self.stages) \
                    else d["out"]
                self.emit_ffn(src=d["x"], dst=dst, tag="f1", conv=True)
                p.barrier()
                for nm in ("wupb", "wdownb", "shupb", "shdownb"):
                    tap = self.mktap(nm, list(d[nm].shape), BF16)
                    if tap is not None:
                        p.dma("gpsimd", tap, d[nm])
                if self.dbg:
                    p.barrier()
                tap = self.mktap("x1", (T, D))
                if tap is not None:
                    p.dma("gpsimd", tap, d["x1"])
                    p.barrier()
            if "attn" in self.stages:
                dst = d["x2"] if "ffn2" in self.stages else d["out"]
                self.emit_attn(src=d["x1"] if "ffn1" in self.stages else d["x"],
                               dst=dst)
                p.barrier()
                tap = self.mktap("x2", (T, D))
                if tap is not None and "ffn2" in self.stages:
                    p.dma("gpsimd", tap, d["x2"])
                    p.barrier()
            if "ffn2" in self.stages:
                src = d["x2"] if "attn" in self.stages else (
                    d["x1"] if "ffn1" in self.stages else d["x"])
                self.emit_ffn(src=src, dst=d["out"], tag="f2",
                              conv="ffn1" not in self.stages)
                p.barrier()

            sems, dsems = {}, {}
            for e in ENGS:
                sems[e] = self.es.enter_context(nc.semaphore("sem_" + e))
                dsems[e] = self.es.enter_context(nc.semaphore("dsem_" + e))
            block = self.es.enter_context(nc.Block())
            p.emit(block, sems, dsems)
        return nc

    def emit_ffn(self, src, dst, tag, conv):
        nc, p, d, cs = self.nc, self.p, self.d, self.cs
        with contextlib.ExitStack() as es:
            A = lambda *a, **k: es.enter_context(nc.sbuf_tensor(*a, **k))
            pa = es.enter_context(nc.psum_tensor(f"{tag}_pa", [128, 8, 512], F32))
            bk = Banks(pa)
            s_x = A(f"{tag}_x", [128, 2, D], F32)
            s_mv = A(f"{tag}_mv", [128, NTQ, 2], F32)
            s_bn6 = A(f"{tag}_bn6", [128, 4, 6], F32)
            s_rstd = A(f"{tag}_rstd", [128, NTQ], F32)
            s_ln = A(f"{tag}_ln", [128, D], BF16)
            s_lnT = A(f"{tag}_lnT", [128, KC, TQ], BF16)
            s_sc = A(f"{tag}_sc", [16, TQ], F32)
            s_sct = A(f"{tag}_sct", [128, NTQ, 16], F32)
            s_c = A(f"{tag}_c", [128, NTQ, 16], F32)
            s_rc = A(f"{tag}_rc", [128, NTQ, 16], BF16)
            s_rcT = A(f"{tag}_rcT", [16, TQ], BF16)
            s_cbrow = A(f"{tag}_cbrow", [1, TQ], BF16)
            s_sp = A(f"{tag}_sp", [128, 16], F32)
            s_mtb = A(f"{tag}_mtb", [128, 16], F32)
            s_m1 = A(f"{tag}_m1", [128, 16], F32)
            s_m2 = A(f"{tag}_m2", [128, 16], F32)
            s_mw = A(f"{tag}_mw", [128, 16], F32)
            s_gs = A(f"{tag}_gs", [128, 16], F32)
            s_col = A(f"{tag}_col", [128, 8], F32)
            s_moesh = A(f"{tag}_moesh", [128, KC * DI], BF16)
            s_wup = s_moesh[:].rearrange("p (kc i) -> p kc i", kc=KC)
            s_shd = s_moesh[:].rearrange("p (s ic dd) -> p s ic dd", s=2, ic=4)
            s_wdn = A(f"{tag}_wdn", [128, KI, D], BF16)
            s_a = A(f"{tag}_a", [128, KI, TQ], BF16)
            s_cb = A(f"{tag}_cb", [128, TQ], BF16)
            s_acc = A(f"{tag}_acc", [128, NTQ, D], BF16)
            s_shw = A(f"{tag}_shw", [128, 2, KC, 128], BF16)
            s_a2 = A(f"{tag}_a2", [128, 32, 256], BF16)

            fr = {nm: [] for nm in ("lnT", "acc", "a", "a2", "cb", "wup", "wdn",
                                    "ln", "sc", "sct", "c", "rc", "rcT",
                                    "bn6", "rstd")}
            fr["x"] = [[], []]
            fr["o"] = [[], []]
            fr["shw"] = [[], []]
            fr["shd"] = [[], []]
            wq = "scalar"

            def load_wup(e, waits):
                key = ("up", e)
                if key not in self.wb:
                    t = p.dma("gpsimd", s_wup,
                              d["wup"][e].rearrange("(kc p) i -> p kc i", p=128),
                              waits=list(waits) + self._wbup_prev)
                    self.wb[key] = p.dma(
                        "sync", d["wupb"][e].rearrange("(kc p) i -> p kc i", p=128),
                        s_wup, waits=[t])
                    self._wbup_prev = [self.wb[key]]
                else:
                    t = p.dma(wq, s_wup,
                              d["wupb"][e].rearrange("(kc p) i -> p kc i", p=128),
                              waits=list(waits) + [self.wb[key]])
                return t

            def load_wdn(e, waits):
                key = ("dn", e)
                if key not in self.wb:
                    t = p.dma("gpsimd", s_wdn[:],
                              d["wdown"][e].rearrange("(ic p) dd -> p ic dd", p=128),
                              waits=list(waits) + self._wbdn_prev)
                    self.wb[key] = p.dma(
                        "sync", d["wdownb"][e].rearrange("(ic p) dd -> p ic dd", p=128),
                        s_wdn[:], waits=[t])
                    self._wbdn_prev = [self.wb[key]]
                else:
                    t = p.dma(wq, s_wdn[:],
                              d["wdownb"][e].rearrange("(ic p) dd -> p ic dd", p=128),
                              waits=list(waits) + [self.wb[key]])
                return t

            self._wbup_prev = []
            self._wbdn_prev = []
            t_wup = load_wup(0, [])
            t_wdn = load_wdn(0, [])

            for q in range(NQ):
                conv_q = conv and q == 0
                vtr = bk.bf16(0, 2).rearrange("p (c t) -> p c t", c=KC)
                vrt = bk.a[0:16, 2:4, :].rearrange("p a b -> p (a b)") \
                        .rearrange("p (t c) -> p t c", t=NTQ)
                t_rt = [None] * NTQ
                t_lnT_ev = []
                for t in range(NTQ):
                    tg = q * NTQ + t
                    xbuf = s_x[:, t % 2, :]
                    t_x = p.dma("sync", xbuf, src[tg * 128:(tg + 1) * 128, :],
                                waits=fr["x"][t % 2])
                    fr["x"][t % 2] = []
                    tb = None
                    xw = [t_x] + ([Tok(("d", "sync"), t_x.val + 16)]
                                  if t == 0 else [])
                    for bc in range(4):
                        tb = p.op("vector", "bn_stats", s_bn6[:, bc, :],
                                  xbuf[:, bc * 512:(bc + 1) * 512],
                                  waits=(xw + fr["bn6"]) if bc == 0 else [tb])
                    fr["bn6"] = []
                    t_bn = p.op("vector", "bn_aggr", s_mv[:, t, :],
                                s_bn6[:].rearrange("p a b -> p (a b)"), waits=[tb])
                    fr["bn6"] = [t_bn]
                    t_l = p.op("scalar", "activation", s_rstd[:, t:t + 1],
                               s_mv[:, t, 1:2], AF.Ln, bias=self.s_eps[:],
                               scale=1.0, waits=[t_bn, self.t_eps] + fr["rstd"])
                    t_r = p.op("scalar", "activation", s_rstd[:, t:t + 1],
                               s_rstd[:, t:t + 1], AF.Exp, scale=-0.5, waits=[t_l])
                    t_n = p.op("vector", "tensor_scalar", s_ln[:], xbuf,
                               s_mv[:, t, 0:1], s_rstd[:, t:t + 1],
                               op0=ALU.subtract, op1=ALU.mult,
                               waits=[t_r, t_bn] + fr["ln"])
                    fr["x"][t % 2].append(t_n)
                    t_trs = None
                    for c in range(KC):
                        t_trs = p.op("tensor", "transpose", vtr[:, c, :],
                                     s_ln[:, c * 128:(c + 1) * 128], cs["c_ident"][:],
                                     waits=([t_n] + bk.waits(0, 2)) if c == 0 else [],
                                     inc=(c == KC - 1))
                    fr["ln"] = [t_trs]
                    t_ev = p.op("vector", "tensor_copy",
                                s_lnT[:, :, t * 128:(t + 1) * 128], vtr[:],
                                waits=[t_trs] + (fr["lnT"] if t == 0 else []))
                    bk.set_free(0, 2, [t_ev])
                    t_lnT_ev.append(t_ev)
                    tm = None
                    for k in range(KC):
                        tm = p.op("tensor", "matmul", vrt[:, t, :],
                                  self.s_rwT[:, k, :],
                                  s_lnT[:, k, t * 128:(t + 1) * 128],
                                  start=(k == 0), stop=(k == KC - 1),
                                  waits=([t_ev, self.t_rwT] + bk.waits(2, 2))
                                  if k == 0 else [],
                                  inc=(k == KC - 1))
                    if k == KC - 1:
                        bkrt_done = tm
                    t_rt[t] = tm
                fr["lnT"] = []
                t_sig = p.op("scalar", "activation", s_sc[:],
                             bk.a[0:16, 2:4, :].rearrange("p a b -> p (a b)"),
                             AF.Sigmoid, waits=t_rt + fr["sc"])
                bk.set_free(2, 2, [t_sig])

                vsct = bk.f32(4, 1)
                prev_topk = None
                t_c_toks = []
                for t in range(NTQ):
                    t_tr = p.op("tensor", "transpose", vsct[:, 0:16],
                                s_sc[:, t * 128:(t + 1) * 128],
                                cs["c_identf"][0:16, 0:16],
                                waits=[t_sig] + bk.waits(4, 1))
                    t_st = p.op("vector", "tensor_copy", s_sct[:, t, :],
                                vsct[:, 0:16],
                                waits=[t_tr] + (fr["sct"] if t == 0 else []))
                    bk.set_free(4, 1, [t_st])
                    prev_topk = self._topk(p, cs, s_sct[:, t, :], s_sp, s_mtb,
                                           s_m1, s_m2, s_mw, s_gs, s_col,
                                           s_c[:, t, :], t_st, prev_topk,
                                           fr["c"] if t == 0 else [])
                    t_c_toks.append(prev_topk)
                fr["sct"] = []
                fr["c"] = []
                t_lc = p.op("scalar", "activation",
                            s_c[:].rearrange("p a b -> p (a b)"),
                            s_c[:].rearrange("p a b -> p (a b)"), AF.Ln,
                            waits=t_c_toks)
                t_rc = p.op("scalar", "activation",
                            s_rc[:].rearrange("p a b -> p (a b)"),
                            s_c[:].rearrange("p a b -> p (a b)"), AF.Exp, scale=0.5,
                            waits=[t_lc] + fr["rc"])
                fr["rc"] = []
                vrcT = bk.bf16(4, 1)
                t_rcT_ev = []
                for t in range(NTQ):
                    t_tr = p.op("tensor", "transpose", vrcT[0:16, 0:128],
                                s_rc[:, t, :], cs["c_ident"][:],
                                waits=[t_rc] + bk.waits(4, 1))
                    t_ev = p.op("vector", "tensor_copy",
                                s_rcT[:, t * 128:(t + 1) * 128], vrcT[0:16, 0:128],
                                waits=[t_tr] + (fr["rcT"] if t == 0 else []))
                    bk.set_free(4, 1, [t_ev])
                    t_rcT_ev.append(t_ev)
                fr["rcT"] = []

                for nm, buf, wts, frk in [
                        ("sct", s_sct[:], t_c_toks, "sct"),
                        ("c", s_c[:], [t_lc], "c"),
                        ("rc", s_rc[:], [t_rc], "rc"),
                        ("lnT", s_lnT[:], t_lnT_ev, "lnT"),
                        ("mv", s_mv[:], t_lnT_ev, None)]:
                    tap = self.mktap(f"{tag}_{nm}", [NQ] + list(buf.shape),
                                     buf.dtype)
                    if tap is not None:
                        tt_ = p.dma("sync", tap[q], buf, waits=wts)
                        if frk is not None:
                            fr[frk].append(tt_)

                t_acc = {}
                t_rowev = None
                for e in range(NE):
                    vrow = bk.a[0:1, 6:8, :].rearrange("p a b -> p (a b)")
                    trow = []
                    for g in range(2):
                        trow.append(p.op(
                            "tensor", "matmul",
                            vrow[:, g * 512:(g + 1) * 512],
                            cs["c_ident"][0:16, e:e + 1],
                            s_rcT[:, g * 512:(g + 1) * 512],
                            start=True, stop=True, skip_group_check=True,
                            waits=(t_rcT_ev + [self.ctoks["c_ident"]] +
                                   bk.waits(6, 2)) if g == 0 else []))
                    t_rowev = p.op("vector", "tensor_copy", s_cbrow[:], vrow[:],
                                   waits=trow + ([t_rowev] if t_rowev else []))
                    bk.set_free(6, 2, [t_rowev])
                    vcb = bk.f32(4, 2)
                    tcb = []
                    for g in range(2):
                        tcb.append(p.op(
                            "tensor", "matmul", vcb[:, g * 512:(g + 1) * 512],
                            cs["c_ones_row"][:],
                            s_cbrow[0:1, g * 512:(g + 1) * 512],
                            start=True, stop=True,
                            waits=([t_rowev] + bk.waits(4, 2)) if g == 0 else []))
                    t_cb_ev = p.op("vector", "tensor_copy", s_cb[:], vcb[:],
                                   waits=tcb + fr["cb"])
                    bk.set_free(4, 2, [t_cb_ev])
                    fr["cb"] = []
                    t_aw = []
                    for ic in range(KI):
                        b0 = 0 if ic % 2 == 0 else 2
                        pu = bk.f32(b0, 2)
                        tm = None
                        for k in range(KC):
                            for g in range(2):
                                w0 = []
                                if k == 0 and g == 0:
                                    w0 = [t_wup] + t_lnT_ev + bk.waits(b0, 2)
                                tm = p.op("tensor", "matmul",
                                          pu[:, g * 512:(g + 1) * 512],
                                          s_wup[:, k, ic * 128:(ic + 1) * 128],
                                          s_lnT[:, k, g * 512:(g + 1) * 512],
                                          start=(k == 0), stop=(k == KC - 1),
                                          waits=w0, inc=(k == KC - 1 and g == 1))
                        aslc = s_a[:, ic, :]
                        e1 = p.op("vector", "tensor_scalar", aslc, pu[:],
                                  0.0, None, op0=ALU.max,
                                  waits=[tm, t_cb_ev] + (fr["a"] if ic == 0 else []))
                        bk.set_free(b0, 2, [e1])
                        e2 = p.op("vector", "tensor_tensor", aslc, aslc, s_cb[:],
                                  op=ALU.mult, waits=[e1])
                        e3 = p.op("vector", "tensor_tensor", aslc, aslc, aslc,
                                  op=ALU.mult, waits=[e2])
                        fr["cb"].append(e2)
                        t_aw.append(e3)
                        if ic == KI - 1:
                            fr["wup"] = [tm]
                    fr["a"] = []
                    if e + 1 < NE:
                        t_wup = load_wup(e + 1, fr["wup"])
                        fr["wup"] = []
                    dn_toks = []
                    for grp in range(16):
                        tsub, dh = grp // 2, grp % 2
                        b0 = 4 if grp % 2 == 0 else 6
                        pdn = bk.f32(b0, 2)
                        tm = None
                        for ic in range(KI):
                            for dg in range(2):
                                w0 = []
                                if ic == 0 and dg == 0:
                                    w0 = [t_wdn, t_aw[KI - 1]] + bk.waits(b0, 2)
                                tm = p.op("tensor", "matmul",
                                          pdn[:, dg * 512:(dg + 1) * 512],
                                          s_a[:, ic, tsub * 128:(tsub + 1) * 128],
                                          s_wdn[:, ic, dh * 1024 + dg * 512:
                                                dh * 1024 + (dg + 1) * 512],
                                          start=(ic == 0), stop=(ic == KI - 1),
                                          waits=w0, inc=(ic == KI - 1 and dg == 1))
                        accslc = s_acc[:, tsub, dh * 1024:(dh + 1) * 1024]
                        if e == 0:
                            tacc = p.op("vector", "tensor_copy", accslc, pdn[:],
                                        waits=[tm] + (fr["acc"] if grp == 0 else []))
                        else:
                            tacc = p.op("vector", "scalar_tensor_tensor", accslc,
                                        pdn[:], 1.0, accslc,
                                        op0=ALU.mult, op1=ALU.add,
                                        waits=[tm, t_acc[grp]])
                        bk.set_free(b0, 2, [tacc])
                        t_acc[grp] = tacc
                        dn_toks.append(tacc)
                        fr["a"].append(tacc)
                    if e == 0:
                        fr["acc"] = []
                    fr["wdn"] = [dn_toks[-1]]
                    if e + 1 < NE or q + 1 < NQ:
                        t_wdn = load_wdn((e + 1) % NE, fr["wdn"])
                        fr["wdn"] = []

                fr["shd"][0] = fr["shd"][0] + fr["wup"]
                fr["shd"][1] = fr["shd"][1] + fr["wup"]
                fr["wup"] = []
                for g2 in range(4):
                    t_a2 = [None] * 32
                    for i2 in range(32):
                        slot = i2 % 2
                        key = ("shw", i2)
                        if key not in self.wb:
                            t_w = p.dma("gpsimd", s_shw[:, slot, :, :],
                                        d["shup"][:, i2 * 128:(i2 + 1) * 128]
                                        .rearrange("(kc p) i -> p kc i", p=128),
                                        waits=fr["shw"][slot])
                            self.wb[key] = p.dma(
                                "sync", d["shupb"][:, i2 * 128:(i2 + 1) * 128]
                                .rearrange("(kc p) i -> p kc i", p=128),
                                s_shw[:, slot, :, :], waits=[t_w])
                            fr["shw"][slot] = [self.wb[key]]
                        else:
                            t_w = p.dma(wq, s_shw[:, slot, :, :],
                                        d["shupb"][:, i2 * 128:(i2 + 1) * 128]
                                        .rearrange("(kc p) i -> p kc i", p=128),
                                        waits=fr["shw"][slot] + [self.wb[key]])
                            fr["shw"][slot] = []
                        ps = bk.f32(slot, 1)
                        tm = None
                        for k in range(KC):
                            tm = p.op("tensor", "matmul", ps[:, 0:256],
                                      s_shw[:, slot, k, :],
                                      s_lnT[:, k, g2 * 256:(g2 + 1) * 256],
                                      start=(k == 0), stop=(k == KC - 1),
                                      waits=([p.bump(t_w)] + bk.waits(slot, 1) +
                                             t_lnT_ev)
                                      if k == 0 else [],
                                      inc=(k == KC - 1))
                        fr["shw"][slot].append(tm)
                        a2slc = s_a2[:, i2, :]
                        e1 = p.op("vector", "tensor_scalar", a2slc, ps[:, 0:256],
                                  0.0, None, op0=ALU.max,
                                  waits=[tm] + (fr["a2"] if i2 == 0 else []))
                        bk.set_free(slot, 1, [e1])
                        t_a2[i2] = p.op("vector", "tensor_tensor", a2slc, a2slc,
                                        a2slc, op=ALU.mult, waits=[e1])
                    fr["a2"] = []
                    for ssc in range(8):
                        slot = ssc % 2
                        key = ("shd", ssc)
                        if key not in self.wb:
                            t_w = p.dma("gpsimd", s_shd[:, slot, :, :],
                                        d["shdown"][ssc * 512:(ssc + 1) * 512, :]
                                        .rearrange("(ic p) dd -> p ic dd", p=128),
                                        waits=fr["shd"][slot])
                            self.wb[key] = p.dma(
                                "sync", d["shdownb"][ssc * 512:(ssc + 1) * 512, :]
                                .rearrange("(ic p) dd -> p ic dd", p=128),
                                s_shd[:, slot, :, :], waits=[t_w])
                            fr["shd"][slot] = [self.wb[key]]
                        else:
                            t_w = p.dma(wq, s_shd[:, slot, :, :],
                                        d["shdownb"][ssc * 512:(ssc + 1) * 512, :]
                                        .rearrange("(ic p) dd -> p ic dd", p=128),
                                        waits=fr["shd"][slot] + [self.wb[key]])
                            fr["shd"][slot] = []
                        for grp in range(4):
                            tsub, dh = grp // 2, grp % 2
                            b0 = 4 if grp % 2 == 0 else 6
                            pdn = bk.f32(b0, 2)
                            tm = None
                            for j in range(4):
                                for dg in range(2):
                                    w0 = []
                                    if j == 0 and dg == 0:
                                        w0 = ([p.bump(t_w)] +
                                              t_a2[ssc * 4:ssc * 4 + 4] +
                                              bk.waits(b0, 2))
                                    tm = p.op("tensor", "matmul",
                                              pdn[:, dg * 512:(dg + 1) * 512],
                                              s_a2[:, ssc * 4 + j,
                                                   tsub * 128:(tsub + 1) * 128],
                                              s_shd[:, slot, j,
                                                    dh * 1024 + dg * 512:
                                                    dh * 1024 + (dg + 1) * 512],
                                              start=(j == 0), stop=(j == 3),
                                              waits=w0, inc=(j == 3 and dg == 1))
                            ti_acc = g2 * 2 + tsub
                            gi = ti_acc * 2 + dh
                            accslc = s_acc[:, ti_acc, dh * 1024:(dh + 1) * 1024]
                            tacc = p.op("vector", "scalar_tensor_tensor", accslc,
                                        pdn[:], 1.0, accslc,
                                        op0=ALU.mult, op1=ALU.add,
                                        waits=[tm, t_acc[gi]])
                            bk.set_free(b0, 2, [tacc])
                            t_acc[gi] = tacc
                            fr["a2"].append(tacc)
                        fr["shd"][slot].append(tacc)
                    if conv_q and g2 == 0:
                        p.dma("gpsimd", s_shw[:, 0, 0, :], d["c_tri"],
                              waits=fr["shw"][0])
                        fr["shw"][0] = [Tok(("d", "gpsimd"), p.dcnt["gpsimd"])]

                tap = self.mktap(f"{tag}_acc", (NQ, 128, NTQ, D), BF16)
                if tap is not None:
                    tt_ = p.dma("gpsimd", tap[q],
                                s_acc[:].rearrange("p a b -> p (a b)")
                                if False else s_acc[:],
                                waits=list(t_acc.values()))
                    fr["acc"].append(tt_)
                p.dma(wq, s_shw[:, 0, 0, :], d["c_tri"], waits=fr["shw"][0])
                fr["shw"][0] = [Tok(("d", wq), p.dcnt[wq])]
                if q + 1 < NQ:
                    t_wup = load_wup(0, fr["shd"][0] + fr["shd"][1])
                    fr["shd"] = [[], []]

                for t in range(NTQ):
                    tg = q * NTQ + t
                    t_x2 = p.dma("sync", s_x[:, 0, :],
                                 src[tg * 128:(tg + 1) * 128, :],
                                 waits=fr["x"][0])
                    fr["x"][0] = []
                    t_o = p.op("vector", "scalar_tensor_tensor", s_x[:, 1, :],
                               s_acc[:, t, :], 0.5, s_x[:, 0, :],
                               op0=ALU.mult, op1=ALU.add,
                               waits=[t_x2, t_acc[t * 2], t_acc[t * 2 + 1]] +
                                     fr["x"][1])
                    fr["x"][0] = [t_o]
                    fr["acc"].append(t_o)
                    t_st = p.dma("sync", dst[tg * 128:(tg + 1) * 128, :],
                                 s_x[:, 1, :], waits=[t_o])
                    fr["x"][1] = [t_st]
                fr["lnT"] = list(t_acc.values())
                fr["sc"] = t_c_toks[-1:]
                fr["sct"] = t_c_toks[-1:]
                fr["c"] = [t_rc]
                fr["rc"] = t_rcT_ev[-1:]
                fr["rcT"] = [t_cb_ev]
                fr["cb"] = fr["cb"][-1:]
                fr["rstd"] = [t_lnT_ev[-1]]
                fr["bn6"] = fr["bn6"][-1:]

    def _topk(self, p, cs, st, s_sp, s_mtb, s_m1, s_m2, s_mw, s_gs, s_col,
              c_out, t_scores, prev, c_frees):
        W = [t_scores, self.t_bmi] + ([prev] if prev else [])
        t1 = p.op("vector", "tensor_tensor", s_sp[:], st, self.s_bmi[:],
                  op=ALU.add, waits=W)
        t2 = p.op("vector", "tensor_tensor", s_mtb[:], st, self.s_bmi[:],
                  op=ALU.add, waits=W)
        a_, b_, c_, d_ = (s_mtb[:, 0:16:4], s_mtb[:, 1:16:4],
                          s_mtb[:, 2:16:4], s_mtb[:, 3:16:4])
        g = s_gs
        o1 = p.op("vector", "tensor_tensor", g[:, 0:4], a_, b_, op=ALU.max, waits=[t2])
        o2 = p.op("vector", "tensor_tensor", g[:, 4:8], a_, b_, op=ALU.min, waits=[t2])
        o3 = p.op("vector", "tensor_tensor", g[:, 8:12], c_, d_, op=ALU.max, waits=[t2])
        o4 = p.op("vector", "tensor_tensor", g[:, 12:16], c_, d_, op=ALU.min, waits=[t2])
        o5 = p.op("vector", "tensor_tensor", s_mw[:, 0:4], g[:, 0:4], g[:, 8:12],
                  op=ALU.max, waits=[o1, o3])
        o6 = p.op("vector", "tensor_tensor", s_mw[:, 4:8], g[:, 0:4], g[:, 8:12],
                  op=ALU.min, waits=[o1, o3])
        o7 = p.op("vector", "tensor_tensor", s_mw[:, 8:12], g[:, 4:8], g[:, 12:16],
                  op=ALU.max, waits=[o2, o4])
        o8 = p.op("vector", "tensor_tensor", s_mw[:, 8:12], s_mw[:, 4:8],
                  s_mw[:, 8:12], op=ALU.max, waits=[o6, o7])
        o9 = p.op("vector", "scalar_tensor_tensor", g[:, 0:4], s_mw[:, 0:4], 1.0,
                  s_mw[:, 8:12], op0=ALU.mult, op1=ALU.add, waits=[o5, o8])
        o10 = p.op("vector", "tensor_tensor", g[:, 0:4], g[:, 0:4],
                   cs["c_iota4"][:], op=ALU.subtract, waits=[o9])
        gg = lambda i: g[:, i:i + 1]
        q1 = p.op("vector", "tensor_tensor", s_col[:, 0:1], gg(0), gg(2),
                  op=ALU.max, waits=[o10])
        q2 = p.op("vector", "tensor_tensor", s_col[:, 1:2], gg(0), gg(2),
                  op=ALU.min, waits=[o10])
        q3 = p.op("vector", "tensor_tensor", s_col[:, 2:3], gg(1), gg(3),
                  op=ALU.max, waits=[o10])
        q4 = p.op("vector", "tensor_tensor", s_col[:, 3:4], gg(1), gg(3),
                  op=ALU.min, waits=[o10])
        q5 = p.op("vector", "tensor_tensor", s_col[:, 0:1], s_col[:, 0:1],
                  s_col[:, 2:3], op=ALU.min, waits=[q1, q3])
        q6 = p.op("vector", "tensor_tensor", s_col[:, 1:2], s_col[:, 1:2],
                  s_col[:, 3:4], op=ALU.max, waits=[q2, q4])
        q7 = p.op("vector", "tensor_tensor", s_col[:, 0:1], s_col[:, 0:1],
                  s_col[:, 1:2], op=ALU.max, waits=[q5, q6])
        m0 = p.op("vector", "tensor_scalar", g[:, 4:8], g[:, 0:4], s_col[:, 0:1],
                  None, op0=ALU.is_ge, waits=[q7])
        mk = []
        for gi in range(4):
            mk.append(p.op("vector", "tensor_scalar", s_mtb[:, gi * 4:(gi + 1) * 4],
                           s_sp[:, gi * 4:(gi + 1) * 4], g[:, 4 + gi:5 + gi], None,
                           op0=ALU.mult, waits=[m0, t1]))
        tb = p.op("vector", "tensor_tensor", s_mtb[:], s_mtb[:], cs["c_iota16"][:],
                  op=ALU.subtract, waits=mk)
        r1 = p.op("vector", "reduce_max", s_col[:, 1:2], s_mtb[:], axis=AX.X,
                  negate=True, waits=[tb])
        m1 = p.op("vector", "tensor_scalar", s_m1[:], s_mtb[:], s_col[:, 1:2], 0.0,
                  op0=ALU.add, op1=ALU.is_ge, waits=[r1])
        w1a = p.op("vector", "tensor_tensor", s_mw[:], s_m1[:], st, op=ALU.mult,
                   waits=[m1, o9])
        w1 = p.op("vector", "reduce_sum", s_col[:, 2:3], s_mw[:], axis=AX.X,
                  waits=[w1a])
        sup = p.op("vector", "scalar_tensor_tensor", s_mtb[:], s_m1[:], -1e9,
                   s_mtb[:], op0=ALU.mult, op1=ALU.add, waits=[m1])
        r2 = p.op("vector", "reduce_max", s_col[:, 3:4], s_mtb[:], axis=AX.X,
                  negate=True, waits=[sup])
        m2 = p.op("vector", "tensor_scalar", s_m2[:], s_mtb[:], s_col[:, 3:4], 0.0,
                  op0=ALU.add, op1=ALU.is_ge, waits=[r2])
        w2a = p.op("vector", "tensor_tensor", s_mw[:], s_m2[:], st, op=ALU.mult,
                   waits=[m2, w1])
        w2 = p.op("vector", "reduce_sum", s_col[:, 4:5], s_mw[:], axis=AX.X,
                  waits=[w2a])
        dn = p.op("vector", "tensor_tensor", s_col[:, 5:6], s_col[:, 2:3],
                  s_col[:, 4:5], op=ALU.add, waits=[w1, w2])
        dn2 = p.op("vector", "tensor_scalar", s_col[:, 5:6], s_col[:, 5:6], 1e-20,
                   None, op0=ALU.add, waits=[dn])
        di = p.op("vector", "reciprocal", s_col[:, 6:7], s_col[:, 5:6], waits=[dn2])
        wn1 = p.op("vector", "tensor_tensor", s_col[:, 2:3], s_col[:, 2:3],
                   s_col[:, 6:7], op=ALU.mult, waits=[di])
        wn2 = p.op("vector", "tensor_tensor", s_col[:, 4:5], s_col[:, 4:5],
                   s_col[:, 6:7], op=ALU.mult, waits=[di])
        c1 = p.op("vector", "tensor_scalar", c_out, s_m1[:], s_col[:, 2:3], None,
                  op0=ALU.mult, waits=[wn1] + c_frees)
        c2 = p.op("vector", "scalar_tensor_tensor", c_out, s_m2[:], s_col[:, 4:5],
                  c_out, op0=ALU.mult, op1=ALU.add, waits=[wn2, c1])
        return c2

    def emit_attn(self, src, dst):
        nc, p, d, cs = self.nc, self.p, self.d, self.cs
        NG = 8
        oes = contextlib.ExitStack()
        s_ws1T = oes.enter_context(nc.sbuf_tensor("at_ws1T", [128, KC, 16], BF16))
        s_ws2T = oes.enter_context(nc.sbuf_tensor("at_ws2T", [16, H * MH], BF16))
        s_bs2 = oes.enter_context(nc.sbuf_tensor("at_bs2", [64, 16], F32))
        s_g05 = oes.enter_context(nc.sbuf_tensor("at_g05", [1, 128], BF16))
        s_s16T = oes.enter_context(nc.sbuf_tensor("at_s16T", [16, T], BF16))
        with contextlib.ExitStack() as es:
            A = lambda *a, **k: es.enter_context(nc.sbuf_tensor(*a, **k))
            pa = es.enter_context(nc.psum_tensor("at_pp", [128, 8, 512], F32))
            bk = Banks(pa)
            s_wraw = A("at_wraw", [128, 2, D], BF16)
            s_wtr = A("at_wtr", [128, KC, 128], BF16)
            s_x = A("at_x", [128, 2, D], F32)
            s_mv = A("at_mv", [128, 2], F32)
            s_bn6 = A("at_bn6", [128, 4, 6], F32)
            s_rstd = A("at_rstd", [128, 1], F32)
            s_ln = A("at_ln", [128, D], BF16)
            s_stg = A("at_stg", [128, KC, 128], BF16)

            vtr = bk.bf16(0, 2).rearrange("p (c t) -> p c t", c=KC)
            t_wrfree = [[], []]
            t_wtr_free = []
            for wi, (wsrc, wdst) in enumerate(
                    [(d["wq"], d["wqT"]), (d["wk"], d["wkT"]), (d["wv"], d["wvT"]),
                     (d["wg"], d["wgT"]), (d["wo"], d["woT"])]):
                for rc in range(16):
                    sl = rc % 2
                    t_l = p.dma("gpsimd", s_wraw[:, sl, :],
                                wsrc[rc * 128:(rc + 1) * 128, :],
                                waits=t_wrfree[sl])
                    t_lb = t_l if (wi == 4 and rc == 15) else p.bump(t_l)
                    tt = None
                    for c in range(KC):
                        tt = p.op("tensor", "transpose", vtr[:, c, :],
                                  s_wraw[:, sl, c * 128:(c + 1) * 128],
                                  cs["c_ident"][:],
                                  waits=([t_lb, self.ctoks["c_ident"]] +
                                         bk.waits(0, 2)) if c == 0 else [],
                                  inc=(c == KC - 1))
                    t_wrfree[sl] = [tt]
                    t_ev = p.op("vector", "tensor_copy", s_wtr[:], vtr[:],
                                waits=[tt] + t_wtr_free)
                    bk.set_free(0, 2, [t_ev])
                    t_st = p.dma("sync", wdst[:, rc * 128:(rc + 1) * 128]
                                 .rearrange("(c pp) dd -> pp c dd", pp=128),
                                 s_wtr[:], waits=[t_ev])
                    self.wb[("wT", wi, rc)] = t_st
                    t_wtr_free = [t_st]
            s_wr1 = A("at_wr1", [16, D], BF16)
            t_l1 = p.dma("gpsimd", s_wr1[:], d["ws1"])
            vt1 = bk.bf16(2, 1)
            tprev = None
            for c in range(KC):
                tt = p.op("tensor", "transpose", vt1[0:128, 0:16],
                          s_wr1[:, c * 128:(c + 1) * 128],
                          cs["c_ident"][0:16, 0:16],
                          waits=[p.bump(t_l1)] + bk.waits(2, 1))
                tprev = p.op("vector", "tensor_copy", s_ws1T[:, c, :],
                             vt1[0:128, 0:16],
                             waits=[tt] + ([tprev] if tprev else []))
                bk.set_free(2, 1, [tprev])
            t_ws1 = tprev
            s_wr2 = A("at_wr2", [128, 8, 16], BF16)
            t_l2 = p.dma("gpsimd", s_wr2[:],
                         d["ws2"].rearrange("(c pp) s -> pp c s", pp=128))
            tprev = None
            for c in range(8):
                tt = p.op("tensor", "transpose", vt1[0:16, 0:128],
                          s_wr2[:, c, :], cs["c_ident"][:],
                          waits=[p.bump(t_l2)] + bk.waits(2, 1))
                tprev = p.op("vector", "tensor_copy",
                             s_ws2T[:, c * 128:(c + 1) * 128],
                             vt1[0:16, 0:128],
                             waits=[tt] + ([tprev] if tprev else []))
                bk.set_free(2, 1, [tprev])
            t_ws2 = tprev
            t_bs2 = p.dma("sync", s_bs2[:],
                          d["bs2"].rearrange("(hh m) -> m hh", m=64),
                          allow_slow_non_contiguous=True)
            t_gn = p.dma("gpsimd", s_g05[:], d["gnw"].unsqueeze(0))
            t_g05 = p.op("vector", "tensor_scalar", s_g05[:], s_g05[:], 0.5,
                         None, op0=ALU.mult, waits=[t_gn])
            v16 = pa[0:16, 3, 0:128]
            frx = [[], []]
            fr_ln = []
            fr_bn6 = []
            fr_stg = []
            prev_s16ev = None
            for t in range(NT):
                xbuf = s_x[:, t % 2, :]
                t_x = p.dma("sync", xbuf, src[t * 128:(t + 1) * 128, :],
                            waits=frx[t % 2])
                frx[t % 2] = []
                xw = [p.bump(t_x) if t < NT - 1 else t_x]
                tb = None
                for bc in range(4):
                    tb = p.op("vector", "bn_stats", s_bn6[:, bc, :],
                              xbuf[:, bc * 512:(bc + 1) * 512],
                              waits=(xw + fr_bn6) if bc == 0 else [tb])
                fr_bn6 = []
                t_bn = p.op("vector", "bn_aggr", s_mv[:],
                            s_bn6[:].rearrange("p a b -> p (a b)"), waits=[tb])
                fr_bn6 = [t_bn]
                t_ll = p.op("scalar", "activation", s_rstd[:], s_mv[:, 1:2],
                            AF.Ln, bias=self.s_eps[:], scale=1.0,
                            waits=[t_bn, self.t_eps])
                t_r = p.op("scalar", "activation", s_rstd[:], s_rstd[:],
                           AF.Exp, scale=-0.5, waits=[t_ll])
                t_n = p.op("vector", "tensor_scalar", s_ln[:], xbuf,
                           s_mv[:, 0:1], s_rstd[:],
                           op0=ALU.subtract, op1=ALU.mult,
                           waits=[t_r, t_bn] + fr_ln)
                frx[t % 2].append(t_n)
                t_trs = None
                for c in range(KC):
                    t_trs = p.op("tensor", "transpose", vtr[:, c, :],
                                 s_ln[:, c * 128:(c + 1) * 128], cs["c_ident"][:],
                                 waits=([t_n] + bk.waits(0, 2)) if c == 0 else [],
                                 inc=(c == KC - 1))
                fr_ln = [t_trs]
                t_ev = p.op("vector", "tensor_copy", s_stg[:], vtr[:],
                            waits=[t_trs] + fr_stg)
                bk.set_free(0, 2, [t_ev])
                t_st = p.dma("gpsimd", d["ln2T"][:, :, t * 128:(t + 1) * 128]
                             .rearrange("c pp tt -> pp c tt"),
                             s_stg[:], waits=[t_ev])
                self.wb[("ln2T", t)] = t_st
                tm = None
                for k in range(KC):
                    tm = p.op("tensor", "matmul", v16,
                              s_ws1T[:, k, :], s_stg[:, k, :],
                              start=(k == 0), stop=(k == KC - 1),
                              waits=([t_ev, t_ws1] + bk.waits(3, 1))
                              if k == 0 else [],
                              inc=(k == KC - 1))
                t_s16ev = p.op("vector", "tensor_copy",
                               s_s16T[:, t * 128:(t + 1) * 128], v16,
                               waits=[tm] + ([prev_s16ev] if prev_s16ev else []))
                prev_s16ev = t_s16ev
                bk.set_free(3, 1, [t_s16ev])
                fr_stg = [tm, t_st]
            self.t_s16 = prev_s16ev
            self.t_ws2 = t_ws2
            tap = self.mktap("at_s16T", (16, T), BF16)
            if tap is not None:
                p.dma("sync", tap, s_s16T[:], waits=[prev_s16ev])
            tap = self.mktap("at_ln2T", (KC, 128, T), BF16)
            if tap is not None:
                p.dma("gpsimd", tap, d["ln2T"],
                      waits=[self.wb[("ln2T", NT - 1)]])
            tap = self.mktap("at_wqT", (D, D), BF16)
            if tap is not None:
                p.dma("gpsimd", tap, d["wqT"],
                      waits=[self.wb[("wT", 0, rc)] for rc in range(16)])
        p.barrier()

        with contextlib.ExitStack() as es:
            A = lambda *a, **k: es.enter_context(nc.sbuf_tensor(*a, **k))
            pa = es.enter_context(nc.psum_tensor("ah_pp", [128, 8, 512], F32))
            bk = Banks(pa)
            s_cos = A("ah_cos", [128, T], BF16)
            s_sin = A("ah_sin", [128, T], BF16)
            s_cosq = A("ah_cosq", [128, T], BF16)
            s_sinq = A("ah_sinq", [128, T], BF16)
            t_cos = [p.dma("sync", s_cos[:], d["c_cos"]),
                     p.dma("sync", s_sin[:], d["c_sin"]),
                     p.dma("sync", s_cosq[:], d["c_cosq"]),
                     p.dma("sync", s_sinq[:], d["c_sinq"])]
            s_w4 = A("ah_w4", [128, 4, KC, 128], BF16)
            s_l2 = A("ah_l2", [128, 2, 1024], BF16)
            s_qT = A("ah_qT", [128, T], BF16)
            s_kT = A("ah_kT", [128, T], BF16)
            s_gT = A("ah_gT", [128, T], BF16)
            s_th = A("ah_th", [128, T], BF16)
            s_vT = A("ah_vT", [128, T], BF16)
            s_vtok = A("ah_vtok", [128, NT, 128], BF16)
            s_esT = A("ah_esT", [64, T], BF16)
            s_estok = A("ah_estok", [128, NT, MH], BF16)
            s_enztok = A("ah_enztok", [128, NT, MH], BF16)
            s_uT = A("ah_uT", [64, T], BF16)
            s_carry = A("ah_carry", [1, MH], F32)
            s_t1 = A("ah_t1", [128, 1024], F32)
            s_t2 = A("ah_t2", [128, 1024], BF16)
            s_strip = A("ah_strip", [128, 2, 512], BF16)
            s_sf = A("ah_sf", [64, 512], F32)
            s_okf = A("ah_okf", [64, 512], F32)
            s_ok1 = A("ah_ok1", [128, MH], F32)
            s_ok2 = A("ah_ok2", [128, MH], F32)
            s_u = A("ah_u", [128, MH], BF16)
            s_col = A("ah_col", [128, 4], F32)
            s_of = A("ah_of", [128, 512], F32)
            s_sq = A("ah_sq", [128, 512], BF16)
            s_r1 = A("ah_r1", [1, 512], F32)
            s_r2 = A("ah_r2", [1, 512], BF16)
            s_w2g = A("ah_w2g", [128, 512], F32)
            s_og = A("ah_og", [128, 512], BF16)

            fr = {nm: [] for nm in
                  ("w4", "qT", "kT", "gT", "th", "vT", "vtok", "esT", "estok",
                   "enztok", "uT", "carry", "sf", "okf", "ok1", "ok2", "u",
                   "col", "of", "sq", "r1", "r2", "w2g", "og", "t1", "t2")}
            fr["l2"] = [[], []]
            fr["strip"] = [[], []]

            def wload(h, waits):
                toks = []
                for wi, wT in enumerate((d["wqT"], d["wkT"], d["wvT"], d["wgT"])):
                    toks.append(p.dma(
                        "scalar", s_w4[:, wi, :, :],
                        wT[:, h * 128:(h + 1) * 128]
                        .rearrange("(c pp) dd -> pp c dd", pp=128),
                        waits=waits + [self.wb[("wT", wi, h)]]))
                return toks

            t_w4 = wload(0, [])
            t_ogst = {}
            for h in range(H):
                t_es_ev = [None] * NG
                for gsc in range(NG):
                    b = gsc % 2
                    ves = pa[0:64, b, :]
                    tm = p.op("tensor", "matmul", ves,
                              s_ws2T[:, h * MH:(h + 1) * MH],
                              s_s16T[:, gsc * 512:(gsc + 1) * 512],
                              start=True, stop=True,
                              waits=([self.t_ws2, self.t_s16] if gsc == 0 else []) +
                                    bk.waits(b, 1) + (fr["esT"] if gsc == 0 else []))
                    fr["esT"] = []
                    t_e1 = p.op("vector", "tensor_scalar", s_sf[:], ves,
                                s_bs2[0:64, h:h + 1], CLAMP,
                                op0=ALU.add, op1=ALU.min,
                                waits=[tm, t_bs2] + fr["sf"])
                    bk.set_free(b, 1, [t_e1])
                    t_e2 = p.op("vector", "tensor_scalar", s_sf[:], s_sf[:],
                                -CLAMP, None, op0=ALU.max, waits=[t_e1])
                    t_e3 = p.op("scalar", "activation",
                                s_esT[:, gsc * 512:(gsc + 1) * 512], s_sf[:],
                                AF.Exp, waits=[t_e2])
                    fr["sf"] = [t_e3]
                    t_es_ev[gsc] = t_e3
                t_carry_ev = None
                t_et_all = [None] * NT
                for t in range(NT):
                    vestr = bk.bf16(2, 1)
                    tt = p.op("tensor", "transpose", vestr[0:128, 0:MH],
                              s_esT[:, t * 128:(t + 1) * 128],
                              cs["c_ident"][0:64, 0:64],
                              waits=[t_es_ev[t // 4]] + bk.waits(2, 1))
                    t_et = p.op("vector", "tensor_copy", s_estok[:, t, :],
                                vestr[0:128, 0:MH],
                                waits=[tt] + (fr["estok"] if t == 0 else []))
                    bk.set_free(2, 1, [t_et])
                    t_et_all[t] = t_et
                    vcum = pa[:, 3, 0:MH]
                    tm1 = p.op("tensor", "matmul", vcum, cs["c_tri"][:],
                               s_estok[:, t, :], start=True, stop=(t == 0),
                               waits=[t_et, self.ctoks["c_tri"]] + bk.waits(3, 1),
                               inc=(t == 0), skip_group_check=True)
                    if t > 0:
                        tm1 = p.op("tensor", "matmul", vcum,
                                   cs["c_ones_rowf"][:], s_carry[:],
                                   start=False, stop=True,
                                   waits=[t_carry_ev, self.ctoks["c_ones_rowf"]],
                                   skip_group_check=True)
                    t_enz = p.op("vector", "reciprocal", s_ok1[:], vcum,
                                 waits=[tm1] + fr["ok1"])
                    bk.set_free(3, 1, [t_enz])
                    t_enz2 = p.op("vector", "tensor_copy", s_enztok[:, t, :],
                                  s_ok1[:],
                                  waits=[t_enz] + (fr["enztok"] if t == 0 else []))
                    fr["ok1"] = [t_enz2]
                    if t == 0:
                        fr["enztok"] = []
                    vcar = pa[0:1, 4, 0:MH]
                    tm2 = p.op("tensor", "matmul", vcar, cs["c_ones_col"][:],
                               s_estok[:, t, :], start=(t == 0), stop=(t == NT - 1),
                               waits=([t_et, self.ctoks["c_ones_col"]] +
                                      bk.waits(4, 1)) if t == 0 else [t_et],
                               skip_group_check=True)
                    t_carry_ev = p.op("vector", "tensor_copy", s_carry[:], vcar,
                                      waits=[tm2] +
                                            (fr["carry"] if t == 0 else []))
                    if t == 0:
                        fr["carry"] = []
                bk.set_free(4, 1, [t_carry_ev])
                fr["estok"] = []
                fr["carry"] = [tm1]
                t_enz_last = t_enz2
                t_rope = []
                t_pj = []
                for cg in range(4):
                    cg0 = cg * 1024
                    tmm = None
                    t_lds = []
                    for k in range(KC):
                        sl = k % 2
                        t_l = p.dma("sync", s_l2[:, sl, :],
                                    d["ln2T"][k, :, cg0:cg0 + 1024],
                                    waits=fr["l2"][sl] +
                                          [self.wb[("ln2T", NT - 1)]])
                        fr["l2"][sl] = []
                        t_lds.append(t_l)
                        t_lb = p.bump(t_l) if not (k == KC - 1 and cg == 3) \
                            else t_l
                        for wi in range(4):
                            for gh in range(2):
                                b = wi * 2 + gh
                                tmm = p.op(
                                    "tensor", "matmul", pa[:, b, :],
                                    s_w4[:, wi, k, :],
                                    s_l2[:, sl, gh * 512:(gh + 1) * 512],
                                    start=(k == 0), stop=(k == KC - 1),
                                    waits=([t_lb, t_w4[wi]] + bk.waits(b, 1))
                                    if k == 0 else
                                    ([t_lb] if wi == 0 and gh == 0 else []),
                                    inc=(wi == 3 and gh == 1))
                        fr["l2"][sl] = [tmm]
                    t_pj.append(tmm)
                    for wi, (dst_sb, ctab, stab) in enumerate(
                            [(s_qT, s_cosq, s_sinq), (s_kT, s_cos, s_sin)]):
                        vq = bk.f32(wi * 2, 2)
                        e1 = p.op("vector", "tensor_tensor", s_t1[:], vq,
                                  ctab[:, cg0:cg0 + 1024], op=ALU.mult,
                                  waits=[tmm] + t_cos + fr["t1"])
                        e2 = p.op("vector", "tensor_tensor", s_t2[:], vq,
                                  stab[:, cg0:cg0 + 1024], op=ALU.mult,
                                  waits=[tmm] + fr["t2"])
                        bk.set_free(wi * 2, 2, [e1, e2])
                        tsw = None
                        for gh in range(2):
                            tsw = p.op("tensor", "matmul",
                                       pa[:, wi * 2 + gh, :],
                                       cs["c_swap"][:],
                                       s_t2[:, gh * 512:(gh + 1) * 512],
                                       start=True, stop=True,
                                       waits=([e2, self.ctoks["c_swap"]] +
                                              bk.waits(wi * 2, 2))
                                       if gh == 0 else [],
                                       skip_group_check=True,
                                       inc=(gh == 1))
                        e3 = p.op("vector", "tensor_tensor",
                                  dst_sb[:, cg0:cg0 + 1024],
                                  s_t1[:], bk.f32(wi * 2, 2), op=ALU.add,
                                  waits=[tsw, e1] +
                                        (fr["qT" if wi == 0 else "kT"]
                                         if cg == 0 else []))
                        bk.set_free(wi * 2, 2, [e3])
                        fr["t1"] = [e3]
                        fr["t2"] = [tsw]
                        t_rope.append(e3)
                        if cg == 0:
                            fr["qT" if wi == 0 else "kT"] = []
                    ev = p.op("vector", "tensor_copy", s_vT[:, cg0:cg0 + 1024],
                              bk.f32(4, 2),
                              waits=[tmm] + (fr["vT"] if cg == 0 else []))
                    bk.set_free(4, 2, [ev])
                    t_rope.append(ev)
                    if cg == 0:
                        fr["vT"] = []
                    eg = p.op("vector", "tensor_copy", s_gT[:, cg0:cg0 + 1024],
                              bk.f32(6, 2),
                              waits=[tmm] + (fr["gT"] if cg == 0 else []))
                    bk.set_free(6, 2, [eg])
                    t_rope.append(eg)
                    if cg == 0:
                        fr["gT"] = []
                t_gT_done = t_rope[-1]
                t_th = p.op("scalar", "activation", s_th[:], s_gT[:], AF.Tanh,
                            scale=0.5, waits=[t_gT_done] + fr["th"])
                fr["th"] = []
                t_vtok = [None] * NT
                for t in range(NT):
                    vv = bk.bf16(4, 1)
                    tt = p.op("tensor", "transpose", vv[0:128, 0:128],
                              s_vT[:, t * 128:(t + 1) * 128], cs["c_ident"][:],
                              waits=[t_rope[-2] if t == 0 else t_rope[-2]] +
                                    bk.waits(4, 1),
                              inc=True)
                    t_vtok[t] = p.op("vector", "tensor_copy", s_vtok[:, t, :],
                                     vv[0:128, 0:128],
                                     waits=[tt] + (fr["vtok"] if t == 0 else []))
                    bk.set_free(4, 1, [t_vtok[t]])
                fr["vtok"] = []
                if h + 1 < H:
                    t_w4_next = wload(h + 1, t_pj)
                if h == 0:
                    for nm, buf, wts in [
                            ("at_vT", s_vT[:], t_rope),
                            ("at_gT", s_gT[:], t_rope),
                            ("at_esT", s_esT[:], t_es_ev),
                            ("at_enz", s_enztok[:], [t_enz_last]),
                            ("at_qT", s_qT[:], t_rope),
                            ("at_kT", s_kT[:], t_rope),
                            ("at_vtok", s_vtok[:], t_vtok)]:
                        tap = self.mktap(nm, list(buf.shape), BF16)
                        if tap is not None:
                            tt_ = p.dma("sync", tap, buf, waits=wts)
                            for key in ("esT", "enztok", "qT", "kT", "vtok",
                                        "vT", "gT"):
                                fr[key].append(tt_)
                t_uT_ev = []
                t_q_last = t_rope[-4]
                for g in range(NG):
                    t_ok = None
                    for tau in range(4 * g + 4):
                        diag = tau >= 4 * g
                        nd = 512 - (tau - 4 * g) * 128 if diag else 512
                        offl = 512 - nd
                        bA = 5 + (tau % 2)
                        tsm = p.op("tensor", "matmul", pa[:, bA, 0:nd],
                                   s_kT[:, tau * 128:(tau + 1) * 128],
                                   s_qT[:, g * 512 + offl:(g + 1) * 512],
                                   start=True, stop=True,
                                   waits=[t_rope[-4], t_rope[-3], t_q_last] +
                                         bk.waits(bA, 1),
                                   skip_group_check=True)
                        sl = tau % 2
                        evs = []
                        if diag:
                            evs.append(p.op(
                                "vector", "tensor_tensor",
                                s_strip[:, sl, 0:128], pa[:, bA, 0:128],
                                cs["c_tri"][:], op=ALU.mult,
                                waits=[tsm] + fr["strip"][sl]))
                            if nd > 128:
                                evs.append(p.op(
                                    "vector", "tensor_copy",
                                    s_strip[:, sl, 128:nd], pa[:, bA, 128:nd],
                                    waits=[tsm]))
                        else:
                            evs.append(p.op("vector", "tensor_copy",
                                            s_strip[:, sl, 0:nd],
                                            pa[:, bA, 0:nd],
                                            waits=[tsm] + fr["strip"][sl]))
                        bk.set_free(bA, 1, list(evs))
                        t_ok = p.op("tensor", "matmul",
                                    pa[0:64, 0, offl:512],
                                    s_estok[:, tau, :], s_strip[:, sl, 0:nd],
                                    start=(tau == 0), stop=(tau == 4 * g + 3),
                                    waits=evs + [t_et_all[tau]] +
                                          (bk.waits(0, 1) if tau == 0 else []),
                                    skip_group_check=True)
                        fr["strip"][sl] = [t_ok]
                    t_okf = p.op("vector", "tensor_copy", s_okf[:],
                                 pa[0:64, 0, :], waits=[t_ok] + fr["okf"])
                    bk.set_free(0, 1, [t_okf])
                    fr["okf"] = []
                    for j in range(4):
                        ti = 4 * g + j
                        votr = pa[:, 3, 0:MH]
                        ttr = p.op("tensor", "transpose", votr,
                                   s_okf[:, j * 128:(j + 1) * 128],
                                   cs["c_identf"][0:64, 0:64],
                                   waits=[t_okf] + bk.waits(3, 1))
                        z1 = p.op("vector", "tensor_tensor", s_ok1[:], votr,
                                  s_enztok[:, ti, :], op=ALU.mult,
                                  waits=[ttr, t_enz_last] + fr["ok1"])
                        bk.set_free(3, 1, [z1])
                        fr["ok1"] = [z1]
                        z2 = p.op("vector", "reduce_max", s_col[:, 0:1],
                                  s_ok1[:], axis=AX.X, negate=True,
                                  waits=[z1] + fr["col"])
                        fr["col"] = []
                        z3 = p.op("scalar", "activation", s_ok2[:], s_ok1[:],
                                  AF.Exp, bias=s_col[:, 0:1],
                                  waits=[z2] + fr["ok2"])
                        fr["ok2"] = []
                        z4 = p.op("vector", "reduce_sum", s_col[:, 1:2],
                                  s_ok2[:], axis=AX.X, waits=[z3])
                        z5 = p.op("vector", "reciprocal", s_col[:, 2:3],
                                  s_col[:, 1:2], waits=[z4])
                        z6 = p.op("vector", "scalar_tensor_tensor", s_u[:],
                                  s_ok2[:], s_col[:, 2:3], s_enztok[:, ti, :],
                                  op0=ALU.mult, op1=ALU.mult,
                                  waits=[z5] + fr["u"])
                        fr["ok2"] = [z6]
                        fr["col"] = [z6]
                        vutr = bk.bf16(3, 1)
                        tu = p.op("tensor", "transpose", vutr[0:64, 0:128],
                                  s_u[:], cs["c_ident"][:],
                                  waits=[z6] + bk.waits(3, 1))
                        fr["u"] = [tu]
                        uev = p.op("vector", "tensor_copy",
                                   s_uT[:, ti * 128:(ti + 1) * 128],
                                   vutr[0:64, 0:128],
                                   waits=[tu] + (fr["uT"] if ti == 0 else []))
                        bk.set_free(3, 1, [uev])
                        t_uT_ev.append(uev)
                        if ti == 0:
                            fr["uT"] = []
                if h == 0:
                    tap = self.mktap("at_uT", (64, T), BF16)
                    if tap is not None:
                        tt_ = p.dma("sync", tap, s_uT[:], waits=t_uT_ev)
                        fr["uT"].append(tt_)
                for g in range(NG):
                    t_o = None
                    for tau in range(4 * g + 4):
                        diag = tau >= 4 * g
                        nd = 512 - (tau - 4 * g) * 128 if diag else 512
                        offl = 512 - nd
                        bB = 5 + (tau % 2)
                        tsm = p.op("tensor", "matmul", pa[:, bB, 0:nd],
                                   s_esT[:, tau * 128:(tau + 1) * 128],
                                   s_uT[:, g * 512 + offl:(g + 1) * 512],
                                   start=True, stop=True,
                                   waits=[t_uT_ev[min(4 * g + 3, NT - 1)],
                                          t_es_ev[tau // 4]] + bk.waits(bB, 1),
                                   skip_group_check=True)
                        sl = tau % 2
                        evs = []
                        if diag:
                            evs.append(p.op(
                                "vector", "tensor_tensor",
                                s_strip[:, sl, 0:128], pa[:, bB, 0:128],
                                cs["c_tri"][:], op=ALU.mult,
                                waits=[tsm] + fr["strip"][sl]))
                            if nd > 128:
                                evs.append(p.op(
                                    "vector", "tensor_copy",
                                    s_strip[:, sl, 128:nd], pa[:, bB, 128:nd],
                                    waits=[tsm]))
                        else:
                            evs.append(p.op("vector", "tensor_copy",
                                            s_strip[:, sl, 0:nd],
                                            pa[:, bB, 0:nd],
                                            waits=[tsm] + fr["strip"][sl]))
                        bk.set_free(bB, 1, list(evs))
                        t_o = p.op("tensor", "matmul", pa[:, 4, offl:512],
                                   s_vtok[:, tau, :], s_strip[:, sl, 0:nd],
                                   start=(tau == 0), stop=(tau == 4 * g + 3),
                                   waits=evs + [t_vtok[tau]] +
                                         (bk.waits(4, 1) if tau == 0 else []),
                                   skip_group_check=True)
                        fr["strip"][sl] = [t_o]
                    g0 = g * 512
                    t_of = p.op("vector", "tensor_copy", s_of[:], pa[:, 4, :],
                                waits=[t_o] + fr["of"])
                    bk.set_free(4, 1, [t_of])
                    t_sq = p.op("vector", "tensor_tensor", s_sq[:], s_of[:],
                                s_of[:], op=ALU.mult, waits=[t_of] + fr["sq"])
                    tss = p.op("tensor", "matmul", pa[0:1, 7, 0:512],
                               cs["c_ones_col"][:], s_sq[:],
                               start=True, stop=True,
                               waits=[t_sq] + bk.waits(7, 1),
                               skip_group_check=True)
                    fr["sq"] = [tss]
                    tr1 = p.op("scalar", "activation", s_r1[:], pa[0:1, 7, 0:512],
                               AF.Ln, bias=self.s_eps[0:1, :], scale=1.0 / 128.0,
                               waits=[tss] + fr["r1"])
                    bk.set_free(7, 1, [tr1])
                    tr2 = p.op("scalar", "activation", s_r2[:], s_r1[:],
                               AF.Exp, scale=-0.5, waits=[tr1] + fr["r2"])
                    fr["r1"] = [tr2]
                    tgb = p.op("tensor", "matmul", pa[:, 3, 0:512],
                               s_g05[:], s_r2[:], start=True, stop=True,
                               waits=[tr2, t_g05] + bk.waits(3, 1),
                               skip_group_check=True)
                    fr["r2"] = [tgb]
                    t_on = p.op("vector", "tensor_tensor", s_of[:], s_of[:],
                                pa[:, 3, 0:512], op=ALU.mult, waits=[tgb, t_of])
                    bk.set_free(3, 1, [t_on])
                    fr["of"] = [t_on]
                    t_wg = p.op("vector", "scalar_tensor_tensor", s_w2g[:],
                                s_th[:, g0:g0 + 512], 1.0, s_gT[:, g0:g0 + 512],
                                op0=ALU.add, op1=ALU.mult,
                                waits=[t_th] + fr["w2g"])
                    t_og = p.op("vector", "tensor_tensor", s_og[:], s_of[:],
                                s_w2g[:], op=ALU.mult,
                                waits=[t_on, t_wg] + fr["og"])
                    fr["of"] = [t_og]
                    fr["w2g"] = [t_og]
                    t_st = p.dma("sync", d["ogT"][h, :, g0:g0 + 512], s_og[:],
                                 waits=[t_og])
                    t_ogst[(h, g)] = t_st
                    fr["og"] = [t_st]
                fr["esT"] = [t_o]
                fr["estok"] = [t_ok, t_o]
                fr["enztok"] = [z6]
                fr["uT"] = [t_o]
                fr["vtok"] = [t_o]
                fr["qT"] = [t_ok]
                fr["kT"] = [t_ok]
                fr["gT"] = [t_og]
                fr["th"] = [t_og]
                fr["vT"] = t_vtok[-1:]
                fr["w4"] = t_pj
                if h + 1 < H:
                    t_w4 = t_w4_next
        p.barrier()

        with contextlib.ExitStack() as es:
            A = lambda *a, **k: es.enter_context(nc.sbuf_tensor(*a, **k))
            pa = es.enter_context(nc.psum_tensor("aw_pp", [128, 8, 512], F32))
            bk = Banks(pa)
            s_wo = A("aw_wo", [128, KC, D], BF16)
            s_ogl = A("aw_ogl", [128, 2, H, 512], BF16)
            s_x1 = A("aw_x1", [128, 2, D], F32)
            s_out = A("aw_out", [128, 2, D], F32)
            t_wo = []
            for c in range(KC):
                t_wo.append(p.dma(
                    "scalar", s_wo[:, c, :],
                    d["woT"][c * 128:(c + 1) * 128, :],
                    waits=[self.wb[("wT", 4, rc)] for rc in range(16)]
                    if c == 0 else []))
            fr_ogl = [[], []]
            fr_x1 = [[], []]
            fr_out = [[], []]
            for tg5 in range(NG):
                sl = tg5 % 2
                t_ogl = []
                for hh in range(H):
                    t_ogl.append(p.dma(
                        "sync", s_ogl[:, sl, hh, :],
                        d["ogT"][hh, :, tg5 * 512:(tg5 + 1) * 512],
                        waits=(fr_ogl[sl] if hh == 0 else []) +
                              [t_ogst[(hh, tg5)]]))
                fr_ogl[sl] = []
                for ts4 in range(4):
                    t = tg5 * 4 + ts4
                    bsl = (ts4 % 2) * 4
                    tmm = None
                    for hh in range(H):
                        t_oglb = p.bump(t_ogl[hh]) if hh < H - 1 else t_ogl[hh]
                        for dgq in range(4):
                            tmm = p.op(
                                "tensor", "matmul",
                                pa[:, bsl + dgq, :],
                                s_ogl[:, sl, hh, ts4 * 128:(ts4 + 1) * 128],
                                s_wo[:, hh, dgq * 512:(dgq + 1) * 512],
                                start=(hh == 0), stop=(hh == H - 1),
                                waits=([t_oglb, t_wo[hh]] +
                                       (bk.waits(bsl, 4) if hh == 0 and dgq == 0
                                        else []))
                                if hh == 0 or dgq == 0 else [],
                                inc=(hh == H - 1 and dgq == 3))
                    fr_ogl[sl].append(tmm)
                    xsl = t % 2
                    t_x1 = p.dma("scalar", s_x1[:, xsl, :],
                                 src[t * 128:(t + 1) * 128, :],
                                 waits=fr_x1[xsl])
                    t_x1b = p.bump(t_x1) if t < NT - 1 else t_x1
                    t_o2 = p.op("vector", "scalar_tensor_tensor",
                                s_out[:, xsl, :],
                                bk.f32(bsl, 4), 1.0, s_x1[:, xsl, :],
                                op0=ALU.mult, op1=ALU.add,
                                waits=[tmm, t_x1b] + fr_out[xsl])
                    bk.set_free(bsl, 4, [t_o2])
                    fr_x1[xsl] = [t_o2]
                    t_st = p.dma("sync", dst[t * 128:(t + 1) * 128, :],
                                 s_out[:, xsl, :], waits=[t_o2])
                    fr_out[xsl] = [t_st]
        oes.close()


_CACHE = {}


def _build(stages, taps=()):
    key = (tuple(stages), tuple(sorted(taps)))
    if key not in _CACHE:
        kb = KB(stages, taps)
        nc = kb.build()
        _CACHE[key] = (nc, sorted(kb.dbg))
    return _CACHE[key]


def _in_maps(inputs):
    x = np.ascontiguousarray(inputs["L0_attn_hidden_states"], dtype=np.float32)
    base = {
        "wq": inputs["L0_attn_Wq"], "wk": inputs["L0_attn_Wk"],
        "wv": inputs["L0_attn_Wv"], "wg": inputs["L0_attn_Wg"],
        "wo": inputs["L0_attn_Wo"], "ws1": inputs["L0_attn_Ws1"],
        "ws2": inputs["L0_attn_Ws2"], "bs2": inputs["L0_attn_bs2"],
        "gnw": inputs["L0_attn_g_norm_weight"],
        "rw": inputs["L0_ffn_router_weight"], "ebias": inputs["L0_ffn_e_bias"],
        "wup": inputs["L0_ffn_w_up"], "wdown": inputs["L0_ffn_w_down"],
        "shup": inputs["L0_ffn_shared_up"], "shdown": inputs["L0_ffn_shared_down"],
    }
    base = {k: np.ascontiguousarray(v, dtype=np.float32) for k, v in base.items()}
    base.update(CONSTS)
    maps = []
    for b in range(B):
        m = dict(base)
        m["x"] = np.ascontiguousarray(x[b])
        maps.append(m)
    return maps


def kernel(**inputs):
    nc, _ = _build(("ffn1", "attn", "ffn2"))
    maps = _in_maps(inputs)
    res = run_bass_kernel_spmd(nc, maps, core_ids=list(range(B)))
    out = np.stack([np.asarray(res.results[b]["out"]) for b in range(B)], axis=0)
    return out.astype(np.float32)


# version 04
# speedup vs baseline: 1.0194x; 1.0194x over previous
import contextlib

import numpy as np
import ml_dtypes

import concourse.bass as bass
import concourse.mybir as mybir
from concourse.bass_utils import run_bass_kernel_spmd

F32 = mybir.dt.float32
BF16 = mybir.dt.bfloat16
AF = mybir.ActivationFunctionType
ALU = mybir.AluOpType
AX = mybir.AxisListType
BF = ml_dtypes.bfloat16

B, T, D = 8, 4096, 2048
H, DKH, DVH, MH = 16, 128, 128, 64
NE, DI, DSH = 16, 1024, 2048 * 2
NT = T // 128
NQ = 4
TQ = T // NQ
NTQ = TQ // 128
KC = D // 128
KI = DI // 128
EPS = 1e-5
CLAMP = 32.0
ROPE_BASE = 10000.0

ENGS = ("sync", "scalar", "vector", "tensor", "gpsimd")


class Tok:
    __slots__ = ("key", "val")

    def __init__(self, key, val):
        self.key = key
        self.val = val


class Prog:

    def __init__(self):
        self.ops = {e: [] for e in ENGS}
        self.cnt = {e: 0 for e in ENGS}
        self.dcnt = {e: 0 for e in ENGS}

    def op(self, eng, method, *args, waits=(), inc=True, **kw):
        tok = None
        if inc:
            self.cnt[eng] += 1
            tok = Tok(("e", eng), self.cnt[eng])
        self.ops[eng].append(("op", method, args, kw, [w for w in waits if w], tok))
        return tok

    def dma(self, eng, out, in_, waits=(), **kw):
        self.dcnt[eng] += 16
        tok = Tok(("d", eng), self.dcnt[eng])
        self.ops[eng].append(("dma", None, (out, in_), kw, [w for w in waits if w], tok))
        return tok

    def bump(self, tok):
        return Tok(tok.key, tok.val + 16)

    def wait(self, eng, waits):
        self.ops[eng].append(("wait", None, None, None, [w for w in waits if w], None))

    def barrier(self):
        toks = [Tok(("e", e), self.cnt[e]) for e in ENGS if self.cnt[e]] + \
               [Tok(("d", e), self.dcnt[e]) for e in ENGS if self.dcnt[e]]
        for e in ENGS:
            self.wait(e, toks)

    def emit(self, block, sems, dsems):
        for eng in ENGS:
            ops = self.ops[eng]
            if not ops:
                continue

            def body(h, ops=ops, eng=eng):
                waited = {}
                for kind, method, args, kw, waits, tok in ops:
                    for w in waits:
                        if waited.get(w.key, 0) >= w.val:
                            continue
                        sem = sems[w.key[1]] if w.key[0] == "e" else dsems[w.key[1]]
                        h.wait_ge(sem, w.val)
                        waited[w.key] = w.val
                    if kind == "wait":
                        continue
                    if kind == "dma":
                        h.dma_start(out=args[0], in_=args[1], **kw).then_inc(dsems[eng], 16)
                    else:
                        inst = getattr(h, method)(*args, **kw)
                        if tok is not None:
                            inst.then_inc(sems[eng], 1)

            getattr(block, eng)(body)


def host_constants():
    pos = np.arange(T, dtype=np.float64)
    inv_freq = 1.0 / (ROPE_BASE ** (np.arange(0, DKH, 2, dtype=np.float64) / DKH))
    ang = pos[:, None] * inv_freq[None, :]
    cos = np.cos(ang).astype(np.float32).T
    sin = np.sin(ang).astype(np.float32).T
    cos128 = np.concatenate([cos, cos], axis=0)
    sin128 = np.concatenate([sin, sin], axis=0)
    qs = float(DKH) ** -0.5
    sin_pm = np.concatenate([sin, -sin], axis=0)
    swap = np.zeros((128, 128), np.float32)
    swap[np.arange(128), np.arange(128) ^ 64] = 1.0
    return {
        "c_cos": cos128.astype(BF),
        "c_sin": sin_pm.astype(BF),
        "c_cosq": (cos128 * qs).astype(BF),
        "c_sinq": (sin_pm * qs).astype(BF),
        "c_swap": swap.astype(BF),
        "c_tri": np.triu(np.ones((128, 128), np.float32)).astype(BF),
        "c_trif": np.triu(np.ones((128, 128), np.float32)),
        "c_iota16": np.tile(np.arange(16, dtype=np.float32)[None, :] * 1e-7, (128, 1)),
        "c_iota4": np.tile(np.arange(4, dtype=np.float32)[None, :] * 1e-7, (128, 1)),
        "c_ones_row": np.ones((1, 128), np.float32).astype(BF),
        "c_ones_rowf": np.ones((1, 128), np.float32),
        "c_ones_col": np.ones((128, 1), np.float32).astype(BF),
        "c_ident": np.eye(128, dtype=np.float32).astype(BF),
        "c_identf": np.eye(128, dtype=np.float32),
    }


CONSTS = host_constants()


class Banks:

    def __init__(self, arena):
        self.a = arena
        self.free = {b: [] for b in range(8)}

    def waits(self, b0, nb):
        w = []
        for b in range(b0, b0 + nb):
            w += self.free[b]
        return w

    def set_free(self, b0, nb, toks):
        for b in range(b0, b0 + nb):
            self.free[b] = list(toks)

    def f32(self, b0, nb):
        v = self.a[:, b0:b0 + nb, :]
        return v.rearrange("p a b -> p (a b)")

    def bf16(self, b0, nb):
        return self.f32(b0, nb).bitcast(BF16)


class KB:
    def __init__(self, stages=("ffn1", "attn", "ffn2"), taps=()):
        self.stages = stages
        self.taps = set(taps)
        nc = bass.Bass("TRN2", target_bir_lowering=False)
        self.nc = nc
        self.p = Prog()
        self.es = contextlib.ExitStack()
        d = {}
        d["x"] = nc.dram_tensor("x", [T, D], F32, kind="ExternalInput").ap()
        for nm, shp in [
            ("wq", [D, D]), ("wk", [D, D]), ("wv", [D, D]), ("wg", [D, D]),
            ("wo", [D, D]), ("ws1", [16, D]), ("ws2", [H * MH, 16]),
            ("bs2", [H * MH]), ("gnw", [DVH]),
            ("rw", [NE, D]), ("ebias", [NE]),
            ("wup", [NE, D, DI]), ("wdown", [NE, DI, D]),
            ("shup", [D, DSH]), ("shdown", [DSH, D]),
        ]:
            d[nm] = nc.dram_tensor(nm, shp, F32, kind="ExternalInput").ap()
        for nm, arr in CONSTS.items():
            dt = BF16 if arr.dtype == BF else F32
            d[nm] = nc.dram_tensor(nm, list(arr.shape), dt, kind="ExternalInput").ap()
        d["x1"] = nc.dram_tensor("x1", [T, D], F32, kind="Internal").ap()
        d["x2"] = nc.dram_tensor("x2", [T, D], F32, kind="Internal").ap()
        d["ln2T"] = nc.dram_tensor("ln2T", [KC, 128, T], BF16, kind="Internal").ap()
        d["ogT"] = nc.dram_tensor("ogT", [H, 128, T], BF16, kind="Internal").ap()
        d["wupb"] = nc.dram_tensor("wupb", [NE, D, DI], BF16, kind="Internal").ap()
        d["wdownb"] = nc.dram_tensor("wdownb", [NE, DI, D], BF16, kind="Internal").ap()
        d["shupb"] = nc.dram_tensor("shupb", [D, DSH], BF16, kind="Internal").ap()
        d["shdownb"] = nc.dram_tensor("shdownb", [DSH, D], BF16, kind="Internal").ap()
        for nm in ("wqT", "wkT", "wvT", "wgT", "woT"):
            d[nm] = nc.dram_tensor(nm, [D, D], BF16, kind="Internal").ap()
        d["out"] = nc.dram_tensor("out", [T, D], F32, kind="ExternalOutput").ap()
        self.d = d
        self.dbg = {}
        self.wb = {}

    def mktap(self, name, shape, dtype=F32):
        if name in self.taps and name not in self.dbg:
            self.dbg[name] = self.nc.dram_tensor(
                "dbg_" + name, list(shape), dtype, kind="ExternalOutput").ap()
        return self.dbg.get(name)

    def build(self):
        nc, p, d = self.nc, self.p, self.d
        with self.es:
            A = lambda *a, **k: self.es.enter_context(nc.sbuf_tensor(*a, **k))
            cs = {}
            ctoks = {}
            for nm in ("c_tri", "c_trif", "c_iota16", "c_iota4", "c_ones_row",
                       "c_ones_rowf", "c_ones_col", "c_ident", "c_identf",
                       "c_swap"):
                arr = CONSTS[nm]
                dt = BF16 if arr.dtype == BF else F32
                cs[nm] = A("s" + nm, list(arr.shape), dt)
                ctoks[nm] = p.dma("sync", cs[nm][:], d[nm])
            self.cs = cs
            self.ctoks = ctoks
            s_eps = A("s_eps", [128, 1], F32)
            self.t_eps = p.op("vector", "memset", s_eps[:], float(EPS))
            self.s_eps = s_eps

            with nc.psum_tensor("pp_init", [128, 8, 512], F32) as pa:
                bk = Banks(pa)
                s_rwraw = A("s_rwraw", [16, D], BF16)
                t0 = p.dma("gpsimd", s_rwraw[:], d["rw"])
                s_rwT = A("s_rwT", [128, KC, 16], BF16)
                prw = bk.bf16(0, 1).rearrange("p (a b) -> p a b", a=8)
                t_prev = None
                for k in range(KC):
                    tt = p.op("tensor", "transpose", prw[:, k % 8, 0:16],
                              s_rwraw[:, k * 128:(k + 1) * 128],
                              cs["c_ident"][0:16, 0:16],
                              waits=[t0, ctoks["c_ident"]] +
                                    ([t_prev] if k >= 8 else []))
                    t_prev = p.op("vector", "tensor_copy", s_rwT[:, k, :],
                                  prw[:, k % 8, 0:16], waits=[tt])
                self.s_rwT = s_rwT
                self.t_rwT = t_prev

                s_eb = A("s_eb", [1, 16], BF16)
                t_eb = p.dma("gpsimd", s_eb[:], d["ebias"].unsqueeze(0))
                s_bmi = A("s_bmi", [128, 16], F32)
                peb = bk.f32(1, 1)
                tm = p.op("tensor", "matmul", peb[:, 0:16], cs["c_ones_row"][:],
                          s_eb[:], start=True, stop=True,
                          waits=[t_eb, ctoks["c_ones_row"], t_prev])
                self.t_bmi = p.op("vector", "tensor_tensor", s_bmi[:],
                                  peb[:, 0:16], cs["c_iota16"][:],
                                  op=ALU.subtract, waits=[tm, ctoks["c_iota16"]])
                self.s_bmi = s_bmi
            p.barrier()

            if "ffn1" in self.stages:
                dst = d["x1"] if ("attn" in self.stages or "ffn2" in self.stages) \
                    else d["out"]
                self.emit_ffn(src=d["x"], dst=dst, tag="f1", conv=True)
                p.barrier()
                for nm in ("wupb", "wdownb", "shupb", "shdownb"):
                    tap = self.mktap(nm, list(d[nm].shape), BF16)
                    if tap is not None:
                        p.dma("gpsimd", tap, d[nm])
                if self.dbg:
                    p.barrier()
                tap = self.mktap("x1", (T, D))
                if tap is not None:
                    p.dma("gpsimd", tap, d["x1"])
                    p.barrier()
            if "attn" in self.stages:
                dst = d["x2"] if "ffn2" in self.stages else d["out"]
                self.emit_attn(src=d["x1"] if "ffn1" in self.stages else d["x"],
                               dst=dst)
                p.barrier()
                tap = self.mktap("x2", (T, D))
                if tap is not None and "ffn2" in self.stages:
                    p.dma("gpsimd", tap, d["x2"])
                    p.barrier()
            if "ffn2" in self.stages:
                src = d["x2"] if "attn" in self.stages else (
                    d["x1"] if "ffn1" in self.stages else d["x"])
                self.emit_ffn(src=src, dst=d["out"], tag="f2",
                              conv="ffn1" not in self.stages)
                p.barrier()

            sems, dsems = {}, {}
            for e in ENGS:
                sems[e] = self.es.enter_context(nc.semaphore("sem_" + e))
                dsems[e] = self.es.enter_context(nc.semaphore("dsem_" + e))
            block = self.es.enter_context(nc.Block())
            p.emit(block, sems, dsems)
        return nc

    def emit_ffn(self, src, dst, tag, conv):
        nc, p, d, cs = self.nc, self.p, self.d, self.cs
        with contextlib.ExitStack() as es:
            A = lambda *a, **k: es.enter_context(nc.sbuf_tensor(*a, **k))
            pa = es.enter_context(nc.psum_tensor(f"{tag}_pa", [128, 8, 512], F32))
            bk = Banks(pa)
            s_x = A(f"{tag}_x", [128, 2, D], F32)
            s_mv = A(f"{tag}_mv", [128, NTQ, 2], F32)
            s_bn6 = A(f"{tag}_bn6", [128, 4, 6], F32)
            s_rstd = A(f"{tag}_rstd", [128, NTQ], F32)
            s_ln = A(f"{tag}_ln", [128, D], BF16)
            s_lnT = A(f"{tag}_lnT", [128, KC, TQ], BF16)
            s_sc = A(f"{tag}_sc", [16, TQ], F32)
            s_sct = A(f"{tag}_sct", [128, NTQ, 16], F32)
            s_c = A(f"{tag}_c", [128, NTQ, 16], F32)
            s_rc = A(f"{tag}_rc", [128, NTQ, 16], BF16)
            s_rcT = A(f"{tag}_rcT", [16, TQ], BF16)
            s_cbrow = A(f"{tag}_cbrow", [1, TQ], BF16)
            s_sp = A(f"{tag}_sp", [128, 16], F32)
            s_mtb = A(f"{tag}_mtb", [128, 16], F32)
            s_m1 = A(f"{tag}_m1", [128, 16], F32)
            s_m2 = A(f"{tag}_m2", [128, 16], F32)
            s_mw = A(f"{tag}_mw", [128, 16], F32)
            s_gs = A(f"{tag}_gs", [128, 16], F32)
            s_col = A(f"{tag}_col", [128, 8], F32)
            s_moesh = A(f"{tag}_moesh", [128, KC * DI], BF16)
            s_wup = s_moesh[:].rearrange("p (kc i) -> p kc i", kc=KC)
            s_shd = s_moesh[:].rearrange("p (s ic dd) -> p s ic dd", s=2, ic=4)
            s_wdn = A(f"{tag}_wdn", [128, KI, D], BF16)
            s_a = A(f"{tag}_a", [128, KI, TQ], BF16)
            s_cb = A(f"{tag}_cb", [128, TQ], BF16)
            s_acc = A(f"{tag}_acc", [128, NTQ, D], BF16)
            s_shw = A(f"{tag}_shw", [128, 2, KC, 128], BF16)
            s_a2 = A(f"{tag}_a2", [128, 32, 256], BF16)

            fr = {nm: [] for nm in ("lnT", "acc", "a", "a2", "cb", "wup", "wdn",
                                    "ln", "sc", "sct", "c", "rc", "rcT",
                                    "bn6", "rstd")}
            fr["x"] = [[], []]
            fr["o"] = [[], []]
            fr["shw"] = [[], []]
            fr["shd"] = [[], []]
            wq = "scalar"

            def load_wup(e, waits):
                key = ("up", e)
                if key not in self.wb:
                    t = p.dma("gpsimd", s_wup,
                              d["wup"][e].rearrange("(kc p) i -> p kc i", p=128),
                              waits=list(waits) + self._wbup_prev)
                    self.wb[key] = p.dma(
                        "sync", d["wupb"][e].rearrange("(kc p) i -> p kc i", p=128),
                        s_wup, waits=[t])
                    self._wbup_prev = [self.wb[key]]
                else:
                    t = p.dma(wq, s_wup,
                              d["wupb"][e].rearrange("(kc p) i -> p kc i", p=128),
                              waits=list(waits) + [self.wb[key]])
                return t

            def load_wdn(e, waits):
                key = ("dn", e)
                if key not in self.wb:
                    t = p.dma("gpsimd", s_wdn[:],
                              d["wdown"][e].rearrange("(ic p) dd -> p ic dd", p=128),
                              waits=list(waits) + self._wbdn_prev)
                    self.wb[key] = p.dma(
                        "sync", d["wdownb"][e].rearrange("(ic p) dd -> p ic dd", p=128),
                        s_wdn[:], waits=[t])
                    self._wbdn_prev = [self.wb[key]]
                else:
                    t = p.dma(wq, s_wdn[:],
                              d["wdownb"][e].rearrange("(ic p) dd -> p ic dd", p=128),
                              waits=list(waits) + [self.wb[key]])
                return t

            self._wbup_prev = []
            self._wbdn_prev = []
            t_wup = load_wup(0, [])
            t_wdn = load_wdn(0, [])

            for q in range(NQ):
                conv_q = conv and q == 0
                vtr = bk.bf16(0, 2).rearrange("p (c t) -> p c t", c=KC)
                vrt = bk.a[0:16, 2:4, :].rearrange("p a b -> p (a b)") \
                        .rearrange("p (t c) -> p t c", t=NTQ)
                t_rt = [None] * NTQ
                t_lnT_ev = []
                for t in range(NTQ):
                    tg = q * NTQ + t
                    xbuf = s_x[:, t % 2, :]
                    t_x = p.dma("sync", xbuf, src[tg * 128:(tg + 1) * 128, :],
                                waits=fr["x"][t % 2])
                    fr["x"][t % 2] = []
                    tb = None
                    xw = [t_x] + ([Tok(("d", "sync"), t_x.val + 16)]
                                  if t == 0 else [])
                    for bc in range(4):
                        tb = p.op("vector", "bn_stats", s_bn6[:, bc, :],
                                  xbuf[:, bc * 512:(bc + 1) * 512],
                                  waits=(xw + fr["bn6"]) if bc == 0 else [tb])
                    fr["bn6"] = []
                    t_bn = p.op("vector", "bn_aggr", s_mv[:, t, :],
                                s_bn6[:].rearrange("p a b -> p (a b)"), waits=[tb])
                    fr["bn6"] = [t_bn]
                    t_l = p.op("scalar", "activation", s_rstd[:, t:t + 1],
                               s_mv[:, t, 1:2], AF.Ln, bias=self.s_eps[:],
                               scale=1.0, waits=[t_bn, self.t_eps] + fr["rstd"])
                    t_r = p.op("scalar", "activation", s_rstd[:, t:t + 1],
                               s_rstd[:, t:t + 1], AF.Exp, scale=-0.5, waits=[t_l])
                    t_n = p.op("vector", "tensor_scalar", s_ln[:], xbuf,
                               s_mv[:, t, 0:1], s_rstd[:, t:t + 1],
                               op0=ALU.subtract, op1=ALU.mult,
                               waits=[t_r, t_bn] + fr["ln"])
                    fr["x"][t % 2].append(t_n)
                    t_trs = None
                    for c in range(KC):
                        t_trs = p.op("tensor", "transpose", vtr[:, c, :],
                                     s_ln[:, c * 128:(c + 1) * 128], cs["c_ident"][:],
                                     waits=([t_n] + bk.waits(0, 2)) if c == 0 else [],
                                     inc=(c == KC - 1))
                    fr["ln"] = [t_trs]
                    t_ev = p.op("vector", "tensor_copy",
                                s_lnT[:, :, t * 128:(t + 1) * 128], vtr[:],
                                waits=[t_trs] + (fr["lnT"] if t == 0 else []))
                    bk.set_free(0, 2, [t_ev])
                    t_lnT_ev.append(t_ev)
                    tm = None
                    for k in range(KC):
                        tm = p.op("tensor", "matmul", vrt[:, t, :],
                                  self.s_rwT[:, k, :],
                                  s_lnT[:, k, t * 128:(t + 1) * 128],
                                  start=(k == 0), stop=(k == KC - 1),
                                  waits=([t_ev, self.t_rwT] + bk.waits(2, 2))
                                  if k == 0 else [],
                                  inc=(k == KC - 1))
                    if k == KC - 1:
                        bkrt_done = tm
                    t_rt[t] = tm
                fr["lnT"] = []
                t_sig = p.op("scalar", "activation", s_sc[:],
                             bk.a[0:16, 2:4, :].rearrange("p a b -> p (a b)"),
                             AF.Sigmoid, waits=t_rt + fr["sc"])
                bk.set_free(2, 2, [t_sig])

                vsct = bk.f32(4, 1)
                prev_topk = None
                t_c_toks = []
                for t in range(NTQ):
                    t_tr = p.op("tensor", "transpose", vsct[:, 0:16],
                                s_sc[:, t * 128:(t + 1) * 128],
                                cs["c_identf"][0:16, 0:16],
                                waits=[t_sig] + bk.waits(4, 1))
                    t_st = p.op("vector", "tensor_copy", s_sct[:, t, :],
                                vsct[:, 0:16],
                                waits=[t_tr] + (fr["sct"] if t == 0 else []))
                    bk.set_free(4, 1, [t_st])
                    prev_topk = self._topk(p, cs, s_sct[:, t, :], s_sp, s_mtb,
                                           s_m1, s_m2, s_mw, s_gs, s_col,
                                           s_c[:, t, :], t_st, prev_topk,
                                           fr["c"] if t == 0 else [])
                    t_c_toks.append(prev_topk)
                fr["sct"] = []
                fr["c"] = []
                t_lc = p.op("scalar", "activation",
                            s_c[:].rearrange("p a b -> p (a b)"),
                            s_c[:].rearrange("p a b -> p (a b)"), AF.Ln,
                            waits=t_c_toks)
                t_rc = p.op("scalar", "activation",
                            s_rc[:].rearrange("p a b -> p (a b)"),
                            s_c[:].rearrange("p a b -> p (a b)"), AF.Exp, scale=0.5,
                            waits=[t_lc] + fr["rc"])
                fr["rc"] = []
                vrcT = bk.bf16(4, 1)
                t_rcT_ev = []
                for t in range(NTQ):
                    t_tr = p.op("tensor", "transpose", vrcT[0:16, 0:128],
                                s_rc[:, t, :], cs["c_ident"][:],
                                waits=[t_rc] + bk.waits(4, 1))
                    t_ev = p.op("vector", "tensor_copy",
                                s_rcT[:, t * 128:(t + 1) * 128], vrcT[0:16, 0:128],
                                waits=[t_tr] + (fr["rcT"] if t == 0 else []))
                    bk.set_free(4, 1, [t_ev])
                    t_rcT_ev.append(t_ev)
                fr["rcT"] = []

                for nm, buf, wts, frk in [
                        ("sct", s_sct[:], t_c_toks, "sct"),
                        ("c", s_c[:], [t_lc], "c"),
                        ("rc", s_rc[:], [t_rc], "rc"),
                        ("lnT", s_lnT[:], t_lnT_ev, "lnT"),
                        ("mv", s_mv[:], t_lnT_ev, None)]:
                    tap = self.mktap(f"{tag}_{nm}", [NQ] + list(buf.shape),
                                     buf.dtype)
                    if tap is not None:
                        tt_ = p.dma("sync", tap[q], buf, waits=wts)
                        if frk is not None:
                            fr[frk].append(tt_)

                t_acc = {}
                t_rowev = None
                for e in range(NE):
                    vrow = bk.a[0:1, 6:8, :].rearrange("p a b -> p (a b)")
                    trow = []
                    for g in range(2):
                        trow.append(p.op(
                            "tensor", "matmul",
                            vrow[:, g * 512:(g + 1) * 512],
                            cs["c_ident"][0:16, e:e + 1],
                            s_rcT[:, g * 512:(g + 1) * 512],
                            start=True, stop=True, skip_group_check=True,
                            waits=(t_rcT_ev + [self.ctoks["c_ident"]] +
                                   bk.waits(6, 2)) if g == 0 else []))
                    t_rowev = p.op("vector", "tensor_copy", s_cbrow[:], vrow[:],
                                   waits=trow + ([t_rowev] if t_rowev else []))
                    bk.set_free(6, 2, [t_rowev])
                    vcb = bk.f32(4, 2)
                    tcb = []
                    for g in range(2):
                        tcb.append(p.op(
                            "tensor", "matmul", vcb[:, g * 512:(g + 1) * 512],
                            cs["c_ones_row"][:],
                            s_cbrow[0:1, g * 512:(g + 1) * 512],
                            start=True, stop=True,
                            waits=([t_rowev] + bk.waits(4, 2)) if g == 0 else []))
                    t_cb_ev = p.op("vector", "tensor_copy", s_cb[:], vcb[:],
                                   waits=tcb + fr["cb"])
                    bk.set_free(4, 2, [t_cb_ev])
                    fr["cb"] = []
                    t_aw = []
                    for ic in range(KI):
                        b0 = 0 if ic % 2 == 0 else 2
                        pu = bk.f32(b0, 2)
                        tm = None
                        for k in range(KC):
                            for g in range(2):
                                w0 = []
                                if k == 0 and g == 0:
                                    w0 = [t_wup] + t_lnT_ev + bk.waits(b0, 2)
                                tm = p.op("tensor", "matmul",
                                          pu[:, g * 512:(g + 1) * 512],
                                          s_wup[:, k, ic * 128:(ic + 1) * 128],
                                          s_lnT[:, k, g * 512:(g + 1) * 512],
                                          start=(k == 0), stop=(k == KC - 1),
                                          waits=w0, inc=(k == KC - 1 and g == 1))
                        aslc = s_a[:, ic, :]
                        e1 = p.op("vector", "tensor_scalar", aslc, pu[:],
                                  0.0, None, op0=ALU.max,
                                  waits=[tm, t_cb_ev] + (fr["a"] if ic == 0 else []))
                        bk.set_free(b0, 2, [e1])
                        e2 = p.op("vector", "tensor_tensor", aslc, aslc, s_cb[:],
                                  op=ALU.mult, waits=[e1])
                        e3 = p.op("vector", "tensor_tensor", aslc, aslc, aslc,
                                  op=ALU.mult, waits=[e2])
                        fr["cb"].append(e2)
                        t_aw.append(e3)
                        if ic == KI - 1:
                            fr["wup"] = [tm]
                    fr["a"] = []
                    if e + 1 < NE:
                        t_wup = load_wup(e + 1, fr["wup"])
                        fr["wup"] = []
                    dn_toks = []
                    for grp in range(16):
                        tsub, dh = grp // 2, grp % 2
                        b0 = 4 if grp % 2 == 0 else 6
                        pdn = bk.f32(b0, 2)
                        tm = None
                        for ic in range(KI):
                            for dg in range(2):
                                w0 = []
                                if ic == 0 and dg == 0:
                                    w0 = [t_wdn, t_aw[KI - 1]] + bk.waits(b0, 2)
                                tm = p.op("tensor", "matmul",
                                          pdn[:, dg * 512:(dg + 1) * 512],
                                          s_a[:, ic, tsub * 128:(tsub + 1) * 128],
                                          s_wdn[:, ic, dh * 1024 + dg * 512:
                                                dh * 1024 + (dg + 1) * 512],
                                          start=(ic == 0), stop=(ic == KI - 1),
                                          waits=w0, inc=(ic == KI - 1 and dg == 1))
                        accslc = s_acc[:, tsub, dh * 1024:(dh + 1) * 1024]
                        if e == 0:
                            tacc = p.op("vector", "tensor_copy", accslc, pdn[:],
                                        waits=[tm] + (fr["acc"] if grp == 0 else []))
                        else:
                            tacc = p.op("vector", "scalar_tensor_tensor", accslc,
                                        pdn[:], 1.0, accslc,
                                        op0=ALU.mult, op1=ALU.add,
                                        waits=[tm, t_acc[grp]])
                        bk.set_free(b0, 2, [tacc])
                        t_acc[grp] = tacc
                        dn_toks.append(tacc)
                        fr["a"].append(tacc)
                    if e == 0:
                        fr["acc"] = []
                    fr["wdn"] = [dn_toks[-1]]
                    if e + 1 < NE or q + 1 < NQ:
                        t_wdn = load_wdn((e + 1) % NE, fr["wdn"])
                        fr["wdn"] = []

                fr["shd"][0] = fr["shd"][0] + fr["wup"]
                fr["shd"][1] = fr["shd"][1] + fr["wup"]
                fr["wup"] = []
                for g2 in range(4):
                    t_a2 = [None] * 32
                    for i2 in range(32):
                        slot = i2 % 2
                        key = ("shw", i2)
                        if key not in self.wb:
                            t_w = p.dma("gpsimd", s_shw[:, slot, :, :],
                                        d["shup"][:, i2 * 128:(i2 + 1) * 128]
                                        .rearrange("(kc p) i -> p kc i", p=128),
                                        waits=fr["shw"][slot])
                            self.wb[key] = p.dma(
                                "sync", d["shupb"][:, i2 * 128:(i2 + 1) * 128]
                                .rearrange("(kc p) i -> p kc i", p=128),
                                s_shw[:, slot, :, :], waits=[t_w])
                            fr["shw"][slot] = [self.wb[key]]
                        else:
                            t_w = p.dma(wq, s_shw[:, slot, :, :],
                                        d["shupb"][:, i2 * 128:(i2 + 1) * 128]
                                        .rearrange("(kc p) i -> p kc i", p=128),
                                        waits=fr["shw"][slot] + [self.wb[key]])
                            fr["shw"][slot] = []
                        ps = bk.f32(slot, 1)
                        tm = None
                        for k in range(KC):
                            tm = p.op("tensor", "matmul", ps[:, 0:256],
                                      s_shw[:, slot, k, :],
                                      s_lnT[:, k, g2 * 256:(g2 + 1) * 256],
                                      start=(k == 0), stop=(k == KC - 1),
                                      waits=([p.bump(t_w)] + bk.waits(slot, 1) +
                                             t_lnT_ev)
                                      if k == 0 else [],
                                      inc=(k == KC - 1))
                        fr["shw"][slot].append(tm)
                        a2slc = s_a2[:, i2, :]
                        e1 = p.op("vector", "tensor_scalar", a2slc, ps[:, 0:256],
                                  0.0, None, op0=ALU.max,
                                  waits=[tm] + (fr["a2"] if i2 == 0 else []))
                        bk.set_free(slot, 1, [e1])
                        t_a2[i2] = p.op("vector", "tensor_tensor", a2slc, a2slc,
                                        a2slc, op=ALU.mult, waits=[e1])
                    fr["a2"] = []
                    for ssc in range(8):
                        slot = ssc % 2
                        key = ("shd", ssc)
                        if key not in self.wb:
                            t_w = p.dma("gpsimd", s_shd[:, slot, :, :],
                                        d["shdown"][ssc * 512:(ssc + 1) * 512, :]
                                        .rearrange("(ic p) dd -> p ic dd", p=128),
                                        waits=fr["shd"][slot])
                            self.wb[key] = p.dma(
                                "sync", d["shdownb"][ssc * 512:(ssc + 1) * 512, :]
                                .rearrange("(ic p) dd -> p ic dd", p=128),
                                s_shd[:, slot, :, :], waits=[t_w])
                            fr["shd"][slot] = [self.wb[key]]
                        else:
                            t_w = p.dma(wq, s_shd[:, slot, :, :],
                                        d["shdownb"][ssc * 512:(ssc + 1) * 512, :]
                                        .rearrange("(ic p) dd -> p ic dd", p=128),
                                        waits=fr["shd"][slot] + [self.wb[key]])
                            fr["shd"][slot] = []
                        for grp in range(4):
                            tsub, dh = grp // 2, grp % 2
                            b0 = 4 if grp % 2 == 0 else 6
                            pdn = bk.f32(b0, 2)
                            tm = None
                            for j in range(4):
                                for dg in range(2):
                                    w0 = []
                                    if j == 0 and dg == 0:
                                        w0 = ([p.bump(t_w)] +
                                              t_a2[ssc * 4:ssc * 4 + 4] +
                                              bk.waits(b0, 2))
                                    tm = p.op("tensor", "matmul",
                                              pdn[:, dg * 512:(dg + 1) * 512],
                                              s_a2[:, ssc * 4 + j,
                                                   tsub * 128:(tsub + 1) * 128],
                                              s_shd[:, slot, j,
                                                    dh * 1024 + dg * 512:
                                                    dh * 1024 + (dg + 1) * 512],
                                              start=(j == 0), stop=(j == 3),
                                              waits=w0, inc=(j == 3 and dg == 1))
                            ti_acc = g2 * 2 + tsub
                            gi = ti_acc * 2 + dh
                            accslc = s_acc[:, ti_acc, dh * 1024:(dh + 1) * 1024]
                            tacc = p.op("vector", "scalar_tensor_tensor", accslc,
                                        pdn[:], 1.0, accslc,
                                        op0=ALU.mult, op1=ALU.add,
                                        waits=[tm, t_acc[gi]])
                            bk.set_free(b0, 2, [tacc])
                            t_acc[gi] = tacc
                            fr["a2"].append(tacc)
                        fr["shd"][slot].append(tacc)
                    if conv_q and g2 == 0:
                        p.dma("gpsimd", s_shw[:, 0, 0, :], d["c_tri"],
                              waits=fr["shw"][0])
                        fr["shw"][0] = [Tok(("d", "gpsimd"), p.dcnt["gpsimd"])]

                tap = self.mktap(f"{tag}_acc", (NQ, 128, NTQ, D), BF16)
                if tap is not None:
                    tt_ = p.dma("gpsimd", tap[q],
                                s_acc[:].rearrange("p a b -> p (a b)")
                                if False else s_acc[:],
                                waits=list(t_acc.values()))
                    fr["acc"].append(tt_)
                p.dma(wq, s_shw[:, 0, 0, :], d["c_tri"], waits=fr["shw"][0])
                fr["shw"][0] = [Tok(("d", wq), p.dcnt[wq])]
                if q + 1 < NQ:
                    t_wup = load_wup(0, fr["shd"][0] + fr["shd"][1])
                    fr["shd"] = [[], []]

                for t in range(NTQ):
                    tg = q * NTQ + t
                    t_x2 = p.dma("sync", s_x[:, 0, :],
                                 src[tg * 128:(tg + 1) * 128, :],
                                 waits=fr["x"][0])
                    fr["x"][0] = []
                    t_o = p.op("vector", "scalar_tensor_tensor", s_x[:, 1, :],
                               s_acc[:, t, :], 0.5, s_x[:, 0, :],
                               op0=ALU.mult, op1=ALU.add,
                               waits=[t_x2, t_acc[t * 2], t_acc[t * 2 + 1]] +
                                     fr["x"][1])
                    fr["x"][0] = [t_o]
                    fr["acc"].append(t_o)
                    t_st = p.dma("sync", dst[tg * 128:(tg + 1) * 128, :],
                                 s_x[:, 1, :], waits=[t_o])
                    fr["x"][1] = [t_st]
                fr["lnT"] = list(t_acc.values())
                fr["sc"] = t_c_toks[-1:]
                fr["sct"] = t_c_toks[-1:]
                fr["c"] = [t_rc]
                fr["rc"] = t_rcT_ev[-1:]
                fr["rcT"] = [t_cb_ev]
                fr["cb"] = fr["cb"][-1:]
                fr["rstd"] = [t_lnT_ev[-1]]
                fr["bn6"] = fr["bn6"][-1:]

    def _topk(self, p, cs, st, s_sp, s_mtb, s_m1, s_m2, s_mw, s_gs, s_col,
              c_out, t_scores, prev, c_frees):
        W = [t_scores, self.t_bmi] + ([prev] if prev else [])
        t1 = p.op("vector", "tensor_tensor", s_sp[:], st, self.s_bmi[:],
                  op=ALU.add, waits=W)
        t2 = p.op("vector", "tensor_tensor", s_mtb[:], st, self.s_bmi[:],
                  op=ALU.add, waits=W)
        a_, b_, c_, d_ = (s_mtb[:, 0:16:4], s_mtb[:, 1:16:4],
                          s_mtb[:, 2:16:4], s_mtb[:, 3:16:4])
        g = s_gs
        o1 = p.op("vector", "tensor_tensor", g[:, 0:4], a_, b_, op=ALU.max, waits=[t2])
        o2 = p.op("vector", "tensor_tensor", g[:, 4:8], a_, b_, op=ALU.min, waits=[t2])
        o3 = p.op("vector", "tensor_tensor", g[:, 8:12], c_, d_, op=ALU.max, waits=[t2])
        o4 = p.op("vector", "tensor_tensor", g[:, 12:16], c_, d_, op=ALU.min, waits=[t2])
        o5 = p.op("vector", "tensor_tensor", s_mw[:, 0:4], g[:, 0:4], g[:, 8:12],
                  op=ALU.max, waits=[o1, o3])
        o6 = p.op("vector", "tensor_tensor", s_mw[:, 4:8], g[:, 0:4], g[:, 8:12],
                  op=ALU.min, waits=[o1, o3])
        o7 = p.op("vector", "tensor_tensor", s_mw[:, 8:12], g[:, 4:8], g[:, 12:16],
                  op=ALU.max, waits=[o2, o4])
        o8 = p.op("vector", "tensor_tensor", s_mw[:, 8:12], s_mw[:, 4:8],
                  s_mw[:, 8:12], op=ALU.max, waits=[o6, o7])
        o9 = p.op("vector", "scalar_tensor_tensor", g[:, 0:4], s_mw[:, 0:4], 1.0,
                  s_mw[:, 8:12], op0=ALU.mult, op1=ALU.add, waits=[o5, o8])
        o10 = p.op("vector", "tensor_tensor", g[:, 0:4], g[:, 0:4],
                   cs["c_iota4"][:], op=ALU.subtract, waits=[o9])
        gg = lambda i: g[:, i:i + 1]
        q1 = p.op("vector", "tensor_tensor", s_col[:, 0:1], gg(0), gg(2),
                  op=ALU.max, waits=[o10])
        q2 = p.op("vector", "tensor_tensor", s_col[:, 1:2], gg(0), gg(2),
                  op=ALU.min, waits=[o10])
        q3 = p.op("vector", "tensor_tensor", s_col[:, 2:3], gg(1), gg(3),
                  op=ALU.max, waits=[o10])
        q4 = p.op("vector", "tensor_tensor", s_col[:, 3:4], gg(1), gg(3),
                  op=ALU.min, waits=[o10])
        q5 = p.op("vector", "tensor_tensor", s_col[:, 0:1], s_col[:, 0:1],
                  s_col[:, 2:3], op=ALU.min, waits=[q1, q3])
        q6 = p.op("vector", "tensor_tensor", s_col[:, 1:2], s_col[:, 1:2],
                  s_col[:, 3:4], op=ALU.max, waits=[q2, q4])
        q7 = p.op("vector", "tensor_tensor", s_col[:, 0:1], s_col[:, 0:1],
                  s_col[:, 1:2], op=ALU.max, waits=[q5, q6])
        m0 = p.op("vector", "tensor_scalar", g[:, 4:8], g[:, 0:4], s_col[:, 0:1],
                  None, op0=ALU.is_ge, waits=[q7])
        mk = []
        for gi in range(4):
            mk.append(p.op("vector", "tensor_scalar", s_mtb[:, gi * 4:(gi + 1) * 4],
                           s_sp[:, gi * 4:(gi + 1) * 4], g[:, 4 + gi:5 + gi], None,
                           op0=ALU.mult, waits=[m0, t1]))
        tb = p.op("vector", "tensor_tensor", s_mtb[:], s_mtb[:], cs["c_iota16"][:],
                  op=ALU.subtract, waits=mk)
        r1 = p.op("vector", "reduce_max", s_col[:, 1:2], s_mtb[:], axis=AX.X,
                  negate=True, waits=[tb])
        m1 = p.op("vector", "tensor_scalar", s_m1[:], s_mtb[:], s_col[:, 1:2], 0.0,
                  op0=ALU.add, op1=ALU.is_ge, waits=[r1])
        w1a = p.op("vector", "tensor_tensor", s_mw[:], s_m1[:], st, op=ALU.mult,
                   waits=[m1, o9])
        w1 = p.op("vector", "reduce_sum", s_col[:, 2:3], s_mw[:], axis=AX.X,
                  waits=[w1a])
        sup = p.op("vector", "scalar_tensor_tensor", s_mtb[:], s_m1[:], -1e9,
                   s_mtb[:], op0=ALU.mult, op1=ALU.add, waits=[m1])
        r2 = p.op("vector", "reduce_max", s_col[:, 3:4], s_mtb[:], axis=AX.X,
                  negate=True, waits=[sup])
        m2 = p.op("vector", "tensor_scalar", s_m2[:], s_mtb[:], s_col[:, 3:4], 0.0,
                  op0=ALU.add, op1=ALU.is_ge, waits=[r2])
        w2a = p.op("vector", "tensor_tensor", s_mw[:], s_m2[:], st, op=ALU.mult,
                   waits=[m2, w1])
        w2 = p.op("vector", "reduce_sum", s_col[:, 4:5], s_mw[:], axis=AX.X,
                  waits=[w2a])
        dn = p.op("vector", "tensor_tensor", s_col[:, 5:6], s_col[:, 2:3],
                  s_col[:, 4:5], op=ALU.add, waits=[w1, w2])
        dn2 = p.op("vector", "tensor_scalar", s_col[:, 5:6], s_col[:, 5:6], 1e-20,
                   None, op0=ALU.add, waits=[dn])
        di = p.op("vector", "reciprocal", s_col[:, 6:7], s_col[:, 5:6], waits=[dn2])
        wn1 = p.op("vector", "tensor_tensor", s_col[:, 2:3], s_col[:, 2:3],
                   s_col[:, 6:7], op=ALU.mult, waits=[di])
        wn2 = p.op("vector", "tensor_tensor", s_col[:, 4:5], s_col[:, 4:5],
                   s_col[:, 6:7], op=ALU.mult, waits=[di])
        c1 = p.op("vector", "tensor_scalar", c_out, s_m1[:], s_col[:, 2:3], None,
                  op0=ALU.mult, waits=[wn1] + c_frees)
        c2 = p.op("vector", "scalar_tensor_tensor", c_out, s_m2[:], s_col[:, 4:5],
                  c_out, op0=ALU.mult, op1=ALU.add, waits=[wn2, c1])
        return c2

    def emit_attn(self, src, dst):
        nc, p, d, cs = self.nc, self.p, self.d, self.cs
        NG = 8
        oes = contextlib.ExitStack()
        s_ws1T = oes.enter_context(nc.sbuf_tensor("at_ws1T", [128, KC, 16], BF16))
        s_ws2T = oes.enter_context(nc.sbuf_tensor("at_ws2T", [16, H * MH], BF16))
        s_bs2 = oes.enter_context(nc.sbuf_tensor("at_bs2", [64, 16], F32))
        s_g05 = oes.enter_context(nc.sbuf_tensor("at_g05", [1, 128], BF16))
        s_s16T = oes.enter_context(nc.sbuf_tensor("at_s16T", [16, T], BF16))
        with contextlib.ExitStack() as es:
            A = lambda *a, **k: es.enter_context(nc.sbuf_tensor(*a, **k))
            pa = es.enter_context(nc.psum_tensor("at_pp", [128, 8, 512], F32))
            bk = Banks(pa)
            s_wraw = A("at_wraw", [128, 2, D], BF16)
            s_wtr = A("at_wtr", [128, KC, 128], BF16)
            s_x = A("at_x", [128, 2, D], F32)
            s_mv = A("at_mv", [128, 2], F32)
            s_bn6 = A("at_bn6", [128, 4, 6], F32)
            s_rstd = A("at_rstd", [128, 1], F32)
            s_ln = A("at_ln", [128, D], BF16)
            s_stg = A("at_stg", [128, KC, 128], BF16)

            vtr = bk.bf16(0, 2).rearrange("p (c t) -> p c t", c=KC)
            t_wrfree = [[], []]
            t_wtr_free = []
            for wi, (wsrc, wdst) in enumerate(
                    [(d["wq"], d["wqT"]), (d["wk"], d["wkT"]), (d["wv"], d["wvT"]),
                     (d["wg"], d["wgT"]), (d["wo"], d["woT"])]):
                for rc in range(16):
                    sl = rc % 2
                    t_l = p.dma("gpsimd", s_wraw[:, sl, :],
                                wsrc[rc * 128:(rc + 1) * 128, :],
                                waits=t_wrfree[sl])
                    t_lb = t_l if (wi == 4 and rc == 15) else p.bump(t_l)
                    tt = None
                    for c in range(KC):
                        tt = p.op("tensor", "transpose", vtr[:, c, :],
                                  s_wraw[:, sl, c * 128:(c + 1) * 128],
                                  cs["c_ident"][:],
                                  waits=([t_lb, self.ctoks["c_ident"]] +
                                         bk.waits(0, 2)) if c == 0 else [],
                                  inc=(c == KC - 1))
                    t_wrfree[sl] = [tt]
                    t_ev = p.op("vector", "tensor_copy", s_wtr[:], vtr[:],
                                waits=[tt] + t_wtr_free)
                    bk.set_free(0, 2, [t_ev])
                    t_st = p.dma("sync", wdst[:, rc * 128:(rc + 1) * 128]
                                 .rearrange("(c pp) dd -> pp c dd", pp=128),
                                 s_wtr[:], waits=[t_ev])
                    self.wb[("wT", wi, rc)] = t_st
                    t_wtr_free = [t_st]
            s_wr1 = A("at_wr1", [16, D], BF16)
            t_l1 = p.dma("gpsimd", s_wr1[:], d["ws1"])
            vt1 = bk.bf16(2, 1)
            tprev = None
            for c in range(KC):
                tt = p.op("tensor", "transpose", vt1[0:128, 0:16],
                          s_wr1[:, c * 128:(c + 1) * 128],
                          cs["c_ident"][0:16, 0:16],
                          waits=[p.bump(t_l1)] + bk.waits(2, 1))
                tprev = p.op("vector", "tensor_copy", s_ws1T[:, c, :],
                             vt1[0:128, 0:16],
                             waits=[tt] + ([tprev] if tprev else []))
                bk.set_free(2, 1, [tprev])
            t_ws1 = tprev
            s_wr2 = A("at_wr2", [128, 8, 16], BF16)
            t_l2 = p.dma("gpsimd", s_wr2[:],
                         d["ws2"].rearrange("(c pp) s -> pp c s", pp=128))
            tprev = None
            for c in range(8):
                tt = p.op("tensor", "transpose", vt1[0:16, 0:128],
                          s_wr2[:, c, :], cs["c_ident"][:],
                          waits=[p.bump(t_l2)] + bk.waits(2, 1))
                tprev = p.op("vector", "tensor_copy",
                             s_ws2T[:, c * 128:(c + 1) * 128],
                             vt1[0:16, 0:128],
                             waits=[tt] + ([tprev] if tprev else []))
                bk.set_free(2, 1, [tprev])
            t_ws2 = tprev
            t_bs2 = p.dma("sync", s_bs2[:],
                          d["bs2"].rearrange("(hh m) -> m hh", m=64),
                          allow_slow_non_contiguous=True)
            t_gn = p.dma("gpsimd", s_g05[:], d["gnw"].unsqueeze(0))
            t_g05 = p.op("vector", "tensor_scalar", s_g05[:], s_g05[:], 0.5,
                         None, op0=ALU.mult, waits=[t_gn])
            v16 = pa[0:16, 3, 0:128]
            frx = [[], []]
            fr_ln = []
            fr_bn6 = []
            fr_stg = []
            prev_s16ev = None
            for t in range(NT):
                xbuf = s_x[:, t % 2, :]
                t_x = p.dma("sync", xbuf, src[t * 128:(t + 1) * 128, :],
                            waits=frx[t % 2])
                frx[t % 2] = []
                xw = [p.bump(t_x) if t < NT - 1 else t_x]
                tb = None
                for bc in range(4):
                    tb = p.op("vector", "bn_stats", s_bn6[:, bc, :],
                              xbuf[:, bc * 512:(bc + 1) * 512],
                              waits=(xw + fr_bn6) if bc == 0 else [tb])
                fr_bn6 = []
                t_bn = p.op("vector", "bn_aggr", s_mv[:],
                            s_bn6[:].rearrange("p a b -> p (a b)"), waits=[tb])
                fr_bn6 = [t_bn]
                t_ll = p.op("scalar", "activation", s_rstd[:], s_mv[:, 1:2],
                            AF.Ln, bias=self.s_eps[:], scale=1.0,
                            waits=[t_bn, self.t_eps])
                t_r = p.op("scalar", "activation", s_rstd[:], s_rstd[:],
                           AF.Exp, scale=-0.5, waits=[t_ll])
                t_n = p.op("vector", "tensor_scalar", s_ln[:], xbuf,
                           s_mv[:, 0:1], s_rstd[:],
                           op0=ALU.subtract, op1=ALU.mult,
                           waits=[t_r, t_bn] + fr_ln)
                frx[t % 2].append(t_n)
                t_trs = None
                for c in range(KC):
                    t_trs = p.op("tensor", "transpose", vtr[:, c, :],
                                 s_ln[:, c * 128:(c + 1) * 128], cs["c_ident"][:],
                                 waits=([t_n] + bk.waits(0, 2)) if c == 0 else [],
                                 inc=(c == KC - 1))
                fr_ln = [t_trs]
                t_ev = p.op("vector", "tensor_copy", s_stg[:], vtr[:],
                            waits=[t_trs] + fr_stg)
                bk.set_free(0, 2, [t_ev])
                t_st = p.dma("gpsimd", d["ln2T"][:, :, t * 128:(t + 1) * 128]
                             .rearrange("c pp tt -> pp c tt"),
                             s_stg[:], waits=[t_ev])
                self.wb[("ln2T", t)] = t_st
                tm = None
                for k in range(KC):
                    tm = p.op("tensor", "matmul", v16,
                              s_ws1T[:, k, :], s_stg[:, k, :],
                              start=(k == 0), stop=(k == KC - 1),
                              waits=([t_ev, t_ws1] + bk.waits(3, 1))
                              if k == 0 else [],
                              inc=(k == KC - 1))
                t_s16ev = p.op("vector", "tensor_copy",
                               s_s16T[:, t * 128:(t + 1) * 128], v16,
                               waits=[tm] + ([prev_s16ev] if prev_s16ev else []))
                prev_s16ev = t_s16ev
                bk.set_free(3, 1, [t_s16ev])
                fr_stg = [tm, t_st]
            self.t_s16 = prev_s16ev
            self.t_ws2 = t_ws2
            tap = self.mktap("at_s16T", (16, T), BF16)
            if tap is not None:
                p.dma("sync", tap, s_s16T[:], waits=[prev_s16ev])
            tap = self.mktap("at_ln2T", (KC, 128, T), BF16)
            if tap is not None:
                p.dma("gpsimd", tap, d["ln2T"],
                      waits=[self.wb[("ln2T", NT - 1)]])
            tap = self.mktap("at_wqT", (D, D), BF16)
            if tap is not None:
                p.dma("gpsimd", tap, d["wqT"],
                      waits=[self.wb[("wT", 0, rc)] for rc in range(16)])
        p.barrier()

        with contextlib.ExitStack() as es:
            A = lambda *a, **k: es.enter_context(nc.sbuf_tensor(*a, **k))
            pa = es.enter_context(nc.psum_tensor("ah_pp", [128, 8, 512], F32))
            bk = Banks(pa)
            s_cos = A("ah_cos", [128, T], BF16)
            s_sin = A("ah_sin", [128, T], BF16)
            s_cosq = A("ah_cosq", [128, T], BF16)
            s_sinq = A("ah_sinq", [128, T], BF16)
            t_cos = [p.dma("sync", s_cos[:], d["c_cos"]),
                     p.dma("sync", s_sin[:], d["c_sin"]),
                     p.dma("sync", s_cosq[:], d["c_cosq"]),
                     p.dma("sync", s_sinq[:], d["c_sinq"])]
            s_w4 = A("ah_w4", [128, 4, KC, 128], BF16)
            s_l2 = A("ah_l2", [128, 2, 1024], BF16)
            s_qT = A("ah_qT", [128, T], BF16)
            s_kT = A("ah_kT", [128, T], BF16)
            s_gT = A("ah_gT", [128, T], BF16)
            s_th = A("ah_th", [128, T], BF16)
            s_vT = A("ah_vT", [128, T], BF16)
            s_vtok = A("ah_vtok", [128, NT, 128], BF16)
            s_esT = A("ah_esT", [64, T], BF16)
            s_estok = A("ah_estok", [128, NT, MH], BF16)
            s_enztok = A("ah_enztok", [128, NT, MH], BF16)
            s_uT = A("ah_uT", [64, T], BF16)
            s_carry = A("ah_carry", [1, MH], F32)
            s_t1 = A("ah_t1", [128, 1024], F32)
            s_t2 = A("ah_t2", [128, 1024], BF16)
            s_strip = A("ah_strip", [128, 4, 512], BF16)
            s_sf = A("ah_sf", [64, 512], F32)
            s_okf = A("ah_okf", [64, 512], F32)
            s_ok1 = A("ah_ok1", [128, MH], F32)
            s_ok2 = A("ah_ok2", [128, MH], F32)
            s_u = A("ah_u", [128, MH], BF16)
            s_col = A("ah_col", [128, 4], F32)
            s_of = A("ah_of", [128, 512], F32)
            s_sq = A("ah_sq", [128, 512], BF16)
            s_r1 = A("ah_r1", [1, 512], F32)
            s_r2 = A("ah_r2", [1, 512], BF16)
            s_w2g = A("ah_w2g", [128, 512], F32)
            s_og = A("ah_og", [128, 512], BF16)

            fr = {nm: [] for nm in
                  ("w4", "qT", "kT", "gT", "th", "vT", "vtok", "esT", "estok",
                   "enztok", "uT", "carry", "sf", "okf", "ok1", "ok2", "u",
                   "col", "of", "sq", "r1", "r2", "w2g", "og", "t1", "t2")}
            fr["l2"] = [[], []]
            fr["strip"] = [[], [], [], []]

            def wload(h, waits):
                toks = []
                for wi, wT in enumerate((d["wqT"], d["wkT"], d["wvT"], d["wgT"])):
                    toks.append(p.dma(
                        "scalar", s_w4[:, wi, :, :],
                        wT[:, h * 128:(h + 1) * 128]
                        .rearrange("(c pp) dd -> pp c dd", pp=128),
                        waits=waits + [self.wb[("wT", wi, h)]]))
                return toks

            t_w4 = wload(0, [])
            t_ogst = {}
            for h in range(H):
                t_es_ev = [None] * NG
                for gsc in range(NG):
                    b = gsc % 2
                    ves = pa[0:64, b, :]
                    tm = p.op("tensor", "matmul", ves,
                              s_ws2T[:, h * MH:(h + 1) * MH],
                              s_s16T[:, gsc * 512:(gsc + 1) * 512],
                              start=True, stop=True,
                              waits=([self.t_ws2, self.t_s16] if gsc == 0 else []) +
                                    bk.waits(b, 1) + (fr["esT"] if gsc == 0 else []))
                    fr["esT"] = []
                    t_e1 = p.op("vector", "tensor_scalar", s_sf[:], ves,
                                s_bs2[0:64, h:h + 1], CLAMP,
                                op0=ALU.add, op1=ALU.min,
                                waits=[tm, t_bs2] + fr["sf"])
                    bk.set_free(b, 1, [t_e1])
                    t_e2 = p.op("vector", "tensor_scalar", s_sf[:], s_sf[:],
                                -CLAMP, None, op0=ALU.max, waits=[t_e1])
                    t_e3 = p.op("scalar", "activation",
                                s_esT[:, gsc * 512:(gsc + 1) * 512], s_sf[:],
                                AF.Exp, waits=[t_e2])
                    fr["sf"] = [t_e3]
                    t_es_ev[gsc] = t_e3
                t_carry_ev = None
                t_et_all = [None] * NT
                for t in range(NT):
                    vestr = bk.bf16(2, 1)
                    tt = p.op("tensor", "transpose", vestr[0:128, 0:MH],
                              s_esT[:, t * 128:(t + 1) * 128],
                              cs["c_ident"][0:64, 0:64],
                              waits=[t_es_ev[t // 4]] + bk.waits(2, 1))
                    t_et = p.op("vector", "tensor_copy", s_estok[:, t, :],
                                vestr[0:128, 0:MH],
                                waits=[tt] + (fr["estok"] if t == 0 else []))
                    bk.set_free(2, 1, [t_et])
                    t_et_all[t] = t_et
                    vcum = pa[:, 3, 0:MH]
                    tm1 = p.op("tensor", "matmul", vcum, cs["c_tri"][:],
                               s_estok[:, t, :], start=True, stop=(t == 0),
                               waits=[t_et, self.ctoks["c_tri"]] + bk.waits(3, 1),
                               inc=(t == 0), skip_group_check=True)
                    if t > 0:
                        tm1 = p.op("tensor", "matmul", vcum,
                                   cs["c_ones_rowf"][:], s_carry[:],
                                   start=False, stop=True,
                                   waits=[t_carry_ev, self.ctoks["c_ones_rowf"]],
                                   skip_group_check=True)
                    t_enz = p.op("vector", "reciprocal", s_ok1[:], vcum,
                                 waits=[tm1] + fr["ok1"])
                    bk.set_free(3, 1, [t_enz])
                    t_enz2 = p.op("vector", "tensor_copy", s_enztok[:, t, :],
                                  s_ok1[:],
                                  waits=[t_enz] + (fr["enztok"] if t == 0 else []))
                    fr["ok1"] = [t_enz2]
                    if t == 0:
                        fr["enztok"] = []
                    vcar = pa[0:1, 4, 0:MH]
                    tm2 = p.op("tensor", "matmul", vcar, cs["c_ones_col"][:],
                               s_estok[:, t, :], start=(t == 0), stop=(t == NT - 1),
                               waits=([t_et, self.ctoks["c_ones_col"]] +
                                      bk.waits(4, 1)) if t == 0 else [t_et],
                               skip_group_check=True)
                    t_carry_ev = p.op("vector", "tensor_copy", s_carry[:], vcar,
                                      waits=[tm2] +
                                            (fr["carry"] if t == 0 else []))
                    if t == 0:
                        fr["carry"] = []
                bk.set_free(4, 1, [t_carry_ev])
                fr["estok"] = []
                fr["carry"] = [tm1]
                t_enz_last = t_enz2
                t_rope = []
                t_pj = []
                for cg in range(4):
                    cg0 = cg * 1024
                    tmm = None
                    t_lds = []
                    for k in range(KC):
                        sl = k % 2
                        t_l = p.dma("sync", s_l2[:, sl, :],
                                    d["ln2T"][k, :, cg0:cg0 + 1024],
                                    waits=fr["l2"][sl] +
                                          [self.wb[("ln2T", NT - 1)]])
                        fr["l2"][sl] = []
                        t_lds.append(t_l)
                        t_lb = p.bump(t_l) if not (k == KC - 1 and cg == 3) \
                            else t_l
                        for wi in range(4):
                            for gh in range(2):
                                b = wi * 2 + gh
                                tmm = p.op(
                                    "tensor", "matmul", pa[:, b, :],
                                    s_w4[:, wi, k, :],
                                    s_l2[:, sl, gh * 512:(gh + 1) * 512],
                                    start=(k == 0), stop=(k == KC - 1),
                                    waits=([t_lb, t_w4[wi]] + bk.waits(b, 1))
                                    if k == 0 else
                                    ([t_lb] if wi == 0 and gh == 0 else []),
                                    inc=(wi == 3 and gh == 1))
                        fr["l2"][sl] = [tmm]
                    t_pj.append(tmm)
                    for wi, (dst_sb, ctab, stab) in enumerate(
                            [(s_qT, s_cosq, s_sinq), (s_kT, s_cos, s_sin)]):
                        vq = bk.f32(wi * 2, 2)
                        e1 = p.op("vector", "tensor_tensor", s_t1[:], vq,
                                  ctab[:, cg0:cg0 + 1024], op=ALU.mult,
                                  waits=[tmm] + t_cos + fr["t1"])
                        e2 = p.op("vector", "tensor_tensor", s_t2[:], vq,
                                  stab[:, cg0:cg0 + 1024], op=ALU.mult,
                                  waits=[tmm] + fr["t2"])
                        bk.set_free(wi * 2, 2, [e1, e2])
                        tsw = None
                        for gh in range(2):
                            tsw = p.op("tensor", "matmul",
                                       pa[:, wi * 2 + gh, :],
                                       cs["c_swap"][:],
                                       s_t2[:, gh * 512:(gh + 1) * 512],
                                       start=True, stop=True,
                                       waits=([e2, self.ctoks["c_swap"]] +
                                              bk.waits(wi * 2, 2))
                                       if gh == 0 else [],
                                       skip_group_check=True,
                                       inc=(gh == 1))
                        e3 = p.op("vector", "tensor_tensor",
                                  dst_sb[:, cg0:cg0 + 1024],
                                  s_t1[:], bk.f32(wi * 2, 2), op=ALU.add,
                                  waits=[tsw, e1] +
                                        (fr["qT" if wi == 0 else "kT"]
                                         if cg == 0 else []))
                        bk.set_free(wi * 2, 2, [e3])
                        fr["t1"] = [e3]
                        fr["t2"] = [tsw]
                        t_rope.append(e3)
                        if cg == 0:
                            fr["qT" if wi == 0 else "kT"] = []
                    ev = p.op("vector", "tensor_copy", s_vT[:, cg0:cg0 + 1024],
                              bk.f32(4, 2),
                              waits=[tmm] + (fr["vT"] if cg == 0 else []))
                    bk.set_free(4, 2, [ev])
                    t_rope.append(ev)
                    if cg == 0:
                        fr["vT"] = []
                    eg = p.op("vector", "tensor_copy", s_gT[:, cg0:cg0 + 1024],
                              bk.f32(6, 2),
                              waits=[tmm] + (fr["gT"] if cg == 0 else []))
                    bk.set_free(6, 2, [eg])
                    t_rope.append(eg)
                    if cg == 0:
                        fr["gT"] = []
                t_gT_done = t_rope[-1]
                t_th = p.op("scalar", "activation", s_th[:], s_gT[:], AF.Tanh,
                            scale=0.5, waits=[t_gT_done] + fr["th"])
                fr["th"] = []
                t_vtok = [None] * NT
                for t in range(NT):
                    vv = bk.bf16(4, 1)
                    tt = p.op("tensor", "transpose", vv[0:128, 0:128],
                              s_vT[:, t * 128:(t + 1) * 128], cs["c_ident"][:],
                              waits=[t_rope[-2] if t == 0 else t_rope[-2]] +
                                    bk.waits(4, 1),
                              inc=True)
                    t_vtok[t] = p.op("vector", "tensor_copy", s_vtok[:, t, :],
                                     vv[0:128, 0:128],
                                     waits=[tt] + (fr["vtok"] if t == 0 else []))
                    bk.set_free(4, 1, [t_vtok[t]])
                fr["vtok"] = []
                if h + 1 < H:
                    t_w4_next = wload(h + 1, t_pj)
                if h == 0:
                    for nm, buf, wts in [
                            ("at_vT", s_vT[:], t_rope),
                            ("at_gT", s_gT[:], t_rope),
                            ("at_esT", s_esT[:], t_es_ev),
                            ("at_enz", s_enztok[:], [t_enz_last]),
                            ("at_qT", s_qT[:], t_rope),
                            ("at_kT", s_kT[:], t_rope),
                            ("at_vtok", s_vtok[:], t_vtok)]:
                        tap = self.mktap(nm, list(buf.shape), BF16)
                        if tap is not None:
                            tt_ = p.dma("sync", tap, buf, waits=wts)
                            for key in ("esT", "enztok", "qT", "kT", "vtok",
                                        "vT", "gT"):
                                fr[key].append(tt_)
                t_uT_ev = []
                t_q_last = t_rope[-4]
                for g in range(NG):
                    t_ok = None
                    for tau in range(4 * g + 4):
                        diag = tau >= 4 * g
                        nd = 512 - (tau - 4 * g) * 128 if diag else 512
                        offl = 512 - nd
                        bA = 4 + (tau % 4)
                        tsm = p.op("tensor", "matmul", pa[:, bA, 0:nd],
                                   s_kT[:, tau * 128:(tau + 1) * 128],
                                   s_qT[:, g * 512 + offl:(g + 1) * 512],
                                   start=True, stop=True,
                                   waits=[t_rope[-4], t_rope[-3], t_q_last] +
                                         bk.waits(bA, 1),
                                   skip_group_check=True)
                        sl = tau % 4
                        evs = []
                        if diag:
                            evs.append(p.op(
                                "vector", "tensor_tensor",
                                s_strip[:, sl, 0:128], pa[:, bA, 0:128],
                                cs["c_tri"][:], op=ALU.mult,
                                waits=[tsm] + fr["strip"][sl]))
                            if nd > 128:
                                evs.append(p.op(
                                    "vector", "tensor_copy",
                                    s_strip[:, sl, 128:nd], pa[:, bA, 128:nd],
                                    waits=[tsm]))
                        else:
                            evs.append(p.op("vector", "tensor_copy",
                                            s_strip[:, sl, 0:nd],
                                            pa[:, bA, 0:nd],
                                            waits=[tsm] + fr["strip"][sl]))
                        bk.set_free(bA, 1, list(evs))
                        t_ok = p.op("tensor", "matmul",
                                    pa[0:64, 0, offl:512],
                                    s_estok[:, tau, :], s_strip[:, sl, 0:nd],
                                    start=(tau == 0), stop=(tau == 4 * g + 3),
                                    waits=evs + [t_et_all[tau]] +
                                          (bk.waits(0, 1) if tau == 0 else []),
                                    skip_group_check=True)
                        fr["strip"][sl] = [t_ok]
                    t_okf = p.op("vector", "tensor_copy", s_okf[:],
                                 pa[0:64, 0, :], waits=[t_ok] + fr["okf"])
                    bk.set_free(0, 1, [t_okf])
                    fr["okf"] = []
                    for j in range(4):
                        ti = 4 * g + j
                        votr = pa[:, 3, 0:MH]
                        ttr = p.op("tensor", "transpose", votr,
                                   s_okf[:, j * 128:(j + 1) * 128],
                                   cs["c_identf"][0:64, 0:64],
                                   waits=[t_okf] + bk.waits(3, 1))
                        z1 = p.op("vector", "tensor_tensor", s_ok1[:], votr,
                                  s_enztok[:, ti, :], op=ALU.mult,
                                  waits=[ttr, t_enz_last] + fr["ok1"])
                        bk.set_free(3, 1, [z1])
                        fr["ok1"] = [z1]
                        z2 = p.op("vector", "reduce_max", s_col[:, 0:1],
                                  s_ok1[:], axis=AX.X, negate=True,
                                  waits=[z1] + fr["col"])
                        fr["col"] = []
                        z3 = p.op("scalar", "activation", s_ok2[:], s_ok1[:],
                                  AF.Exp, bias=s_col[:, 0:1],
                                  waits=[z2] + fr["ok2"])
                        fr["ok2"] = []
                        z4 = p.op("vector", "reduce_sum", s_col[:, 1:2],
                                  s_ok2[:], axis=AX.X, waits=[z3])
                        z5 = p.op("vector", "reciprocal", s_col[:, 2:3],
                                  s_col[:, 1:2], waits=[z4])
                        z6 = p.op("vector", "scalar_tensor_tensor", s_u[:],
                                  s_ok2[:], s_col[:, 2:3], s_enztok[:, ti, :],
                                  op0=ALU.mult, op1=ALU.mult,
                                  waits=[z5] + fr["u"])
                        fr["ok2"] = [z6]
                        fr["col"] = [z6]
                        vutr = bk.bf16(3, 1)
                        tu = p.op("tensor", "transpose", vutr[0:64, 0:128],
                                  s_u[:], cs["c_ident"][:],
                                  waits=[z6] + bk.waits(3, 1))
                        fr["u"] = [tu]
                        uev = p.op("vector", "tensor_copy",
                                   s_uT[:, ti * 128:(ti + 1) * 128],
                                   vutr[0:64, 0:128],
                                   waits=[tu] + (fr["uT"] if ti == 0 else []))
                        bk.set_free(3, 1, [uev])
                        t_uT_ev.append(uev)
                        if ti == 0:
                            fr["uT"] = []
                if h == 0:
                    tap = self.mktap("at_uT", (64, T), BF16)
                    if tap is not None:
                        tt_ = p.dma("sync", tap, s_uT[:], waits=t_uT_ev)
                        fr["uT"].append(tt_)
                for g in range(NG):
                    t_o = None
                    for tau in range(4 * g + 4):
                        diag = tau >= 4 * g
                        nd = 512 - (tau - 4 * g) * 128 if diag else 512
                        offl = 512 - nd
                        bB = (5, 6, 1, 2)[tau % 4]
                        tsm = p.op("tensor", "matmul", pa[:, bB, 0:nd],
                                   s_esT[:, tau * 128:(tau + 1) * 128],
                                   s_uT[:, g * 512 + offl:(g + 1) * 512],
                                   start=True, stop=True,
                                   waits=[t_uT_ev[min(4 * g + 3, NT - 1)],
                                          t_es_ev[tau // 4]] + bk.waits(bB, 1),
                                   skip_group_check=True)
                        sl = tau % 4
                        evs = []
                        if diag:
                            evs.append(p.op(
                                "vector", "tensor_tensor",
                                s_strip[:, sl, 0:128], pa[:, bB, 0:128],
                                cs["c_tri"][:], op=ALU.mult,
                                waits=[tsm] + fr["strip"][sl]))
                            if nd > 128:
                                evs.append(p.op(
                                    "vector", "tensor_copy",
                                    s_strip[:, sl, 128:nd], pa[:, bB, 128:nd],
                                    waits=[tsm]))
                        else:
                            evs.append(p.op("vector", "tensor_copy",
                                            s_strip[:, sl, 0:nd],
                                            pa[:, bB, 0:nd],
                                            waits=[tsm] + fr["strip"][sl]))
                        bk.set_free(bB, 1, list(evs))
                        t_o = p.op("tensor", "matmul", pa[:, 4, offl:512],
                                   s_vtok[:, tau, :], s_strip[:, sl, 0:nd],
                                   start=(tau == 0), stop=(tau == 4 * g + 3),
                                   waits=evs + [t_vtok[tau]] +
                                         (bk.waits(4, 1) if tau == 0 else []),
                                   skip_group_check=True)
                        fr["strip"][sl] = [t_o]
                    g0 = g * 512
                    t_of = p.op("vector", "tensor_copy", s_of[:], pa[:, 4, :],
                                waits=[t_o] + fr["of"])
                    bk.set_free(4, 1, [t_of])
                    t_sq = p.op("vector", "tensor_tensor", s_sq[:], s_of[:],
                                s_of[:], op=ALU.mult, waits=[t_of] + fr["sq"])
                    tss = p.op("tensor", "matmul", pa[0:1, 7, 0:512],
                               cs["c_ones_col"][:], s_sq[:],
                               start=True, stop=True,
                               waits=[t_sq] + bk.waits(7, 1),
                               skip_group_check=True)
                    fr["sq"] = [tss]
                    tr1 = p.op("scalar", "activation", s_r1[:], pa[0:1, 7, 0:512],
                               AF.Ln, bias=self.s_eps[0:1, :], scale=1.0 / 128.0,
                               waits=[tss] + fr["r1"])
                    bk.set_free(7, 1, [tr1])
                    tr2 = p.op("scalar", "activation", s_r2[:], s_r1[:],
                               AF.Exp, scale=-0.5, waits=[tr1] + fr["r2"])
                    fr["r1"] = [tr2]
                    tgb = p.op("tensor", "matmul", pa[:, 3, 0:512],
                               s_g05[:], s_r2[:], start=True, stop=True,
                               waits=[tr2, t_g05] + bk.waits(3, 1),
                               skip_group_check=True)
                    fr["r2"] = [tgb]
                    t_on = p.op("vector", "tensor_tensor", s_of[:], s_of[:],
                                pa[:, 3, 0:512], op=ALU.mult, waits=[tgb, t_of])
                    bk.set_free(3, 1, [t_on])
                    fr["of"] = [t_on]
                    t_wg = p.op("vector", "scalar_tensor_tensor", s_w2g[:],
                                s_th[:, g0:g0 + 512], 1.0, s_gT[:, g0:g0 + 512],
                                op0=ALU.add, op1=ALU.mult,
                                waits=[t_th] + fr["w2g"])
                    t_og = p.op("vector", "tensor_tensor", s_og[:], s_of[:],
                                s_w2g[:], op=ALU.mult,
                                waits=[t_on, t_wg] + fr["og"])
                    fr["of"] = [t_og]
                    fr["w2g"] = [t_og]
                    t_st = p.dma("sync", d["ogT"][h, :, g0:g0 + 512], s_og[:],
                                 waits=[t_og])
                    t_ogst[(h, g)] = t_st
                    fr["og"] = [t_st]
                fr["esT"] = [t_o]
                fr["estok"] = [t_ok, t_o]
                fr["enztok"] = [z6]
                fr["uT"] = [t_o]
                fr["vtok"] = [t_o]
                fr["qT"] = [t_ok]
                fr["kT"] = [t_ok]
                fr["gT"] = [t_og]
                fr["th"] = [t_og]
                fr["vT"] = t_vtok[-1:]
                fr["w4"] = t_pj
                if h + 1 < H:
                    t_w4 = t_w4_next
        p.barrier()

        with contextlib.ExitStack() as es:
            A = lambda *a, **k: es.enter_context(nc.sbuf_tensor(*a, **k))
            pa = es.enter_context(nc.psum_tensor("aw_pp", [128, 8, 512], F32))
            bk = Banks(pa)
            s_wo = A("aw_wo", [128, KC, D], BF16)
            s_ogl = A("aw_ogl", [128, 2, H, 512], BF16)
            s_x1 = A("aw_x1", [128, 2, D], F32)
            s_out = A("aw_out", [128, 2, D], F32)
            t_wo = []
            for c in range(KC):
                t_wo.append(p.dma(
                    "scalar", s_wo[:, c, :],
                    d["woT"][c * 128:(c + 1) * 128, :],
                    waits=[self.wb[("wT", 4, rc)] for rc in range(16)]
                    if c == 0 else []))
            fr_ogl = [[], []]
            fr_x1 = [[], []]
            fr_out = [[], []]
            for tg5 in range(NG):
                sl = tg5 % 2
                t_ogl = []
                for hh in range(H):
                    t_ogl.append(p.dma(
                        "sync", s_ogl[:, sl, hh, :],
                        d["ogT"][hh, :, tg5 * 512:(tg5 + 1) * 512],
                        waits=(fr_ogl[sl] if hh == 0 else []) +
                              [t_ogst[(hh, tg5)]]))
                fr_ogl[sl] = []
                for ts4 in range(4):
                    t = tg5 * 4 + ts4
                    bsl = (ts4 % 2) * 4
                    tmm = None
                    for hh in range(H):
                        t_oglb = p.bump(t_ogl[hh]) if hh < H - 1 else t_ogl[hh]
                        for dgq in range(4):
                            tmm = p.op(
                                "tensor", "matmul",
                                pa[:, bsl + dgq, :],
                                s_ogl[:, sl, hh, ts4 * 128:(ts4 + 1) * 128],
                                s_wo[:, hh, dgq * 512:(dgq + 1) * 512],
                                start=(hh == 0), stop=(hh == H - 1),
                                waits=([t_oglb, t_wo[hh]] +
                                       (bk.waits(bsl, 4) if hh == 0 and dgq == 0
                                        else []))
                                if hh == 0 or dgq == 0 else [],
                                inc=(hh == H - 1 and dgq == 3))
                    fr_ogl[sl].append(tmm)
                    xsl = t % 2
                    t_x1 = p.dma("scalar", s_x1[:, xsl, :],
                                 src[t * 128:(t + 1) * 128, :],
                                 waits=fr_x1[xsl])
                    t_x1b = p.bump(t_x1) if t < NT - 1 else t_x1
                    t_o2 = p.op("vector", "scalar_tensor_tensor",
                                s_out[:, xsl, :],
                                bk.f32(bsl, 4), 1.0, s_x1[:, xsl, :],
                                op0=ALU.mult, op1=ALU.add,
                                waits=[tmm, t_x1b] + fr_out[xsl])
                    bk.set_free(bsl, 4, [t_o2])
                    fr_x1[xsl] = [t_o2]
                    t_st = p.dma("sync", dst[t * 128:(t + 1) * 128, :],
                                 s_out[:, xsl, :], waits=[t_o2])
                    fr_out[xsl] = [t_st]
        oes.close()


_CACHE = {}


def _build(stages, taps=()):
    key = (tuple(stages), tuple(sorted(taps)))
    if key not in _CACHE:
        kb = KB(stages, taps)
        nc = kb.build()
        _CACHE[key] = (nc, sorted(kb.dbg))
    return _CACHE[key]


def _in_maps(inputs):
    x = np.ascontiguousarray(inputs["L0_attn_hidden_states"], dtype=np.float32)
    base = {
        "wq": inputs["L0_attn_Wq"], "wk": inputs["L0_attn_Wk"],
        "wv": inputs["L0_attn_Wv"], "wg": inputs["L0_attn_Wg"],
        "wo": inputs["L0_attn_Wo"], "ws1": inputs["L0_attn_Ws1"],
        "ws2": inputs["L0_attn_Ws2"], "bs2": inputs["L0_attn_bs2"],
        "gnw": inputs["L0_attn_g_norm_weight"],
        "rw": inputs["L0_ffn_router_weight"], "ebias": inputs["L0_ffn_e_bias"],
        "wup": inputs["L0_ffn_w_up"], "wdown": inputs["L0_ffn_w_down"],
        "shup": inputs["L0_ffn_shared_up"], "shdown": inputs["L0_ffn_shared_down"],
    }
    base = {k: np.ascontiguousarray(v, dtype=np.float32) for k, v in base.items()}
    base.update(CONSTS)
    maps = []
    for b in range(B):
        m = dict(base)
        m["x"] = np.ascontiguousarray(x[b])
        maps.append(m)
    return maps


def kernel(**inputs):
    nc, _ = _build(("ffn1", "attn", "ffn2"))
    maps = _in_maps(inputs)
    res = run_bass_kernel_spmd(nc, maps, core_ids=list(range(B)))
    out = np.stack([np.asarray(res.results[b]["out"]) for b in range(B)], axis=0)
    return out.astype(np.float32)
